# version 16
# baseline (speedup 1.0000x reference)
import math
import numpy as np
import concourse.bass as bass
import concourse.mybir as mybir
from concourse.bass_utils import run_bass_kernel_spmd

F32 = mybir.dt.float32
U32 = mybir.dt.uint32
AF = mybir.ActivationFunctionType
ALU = mybir.AluOpType
AX = mybir.AxisListType

LN_EPS = 1e-5
CSHIFT = 8.0
EXP_NEG_C = float(np.float32(math.exp(-CSHIFT)))


class Cfg:
    def __init__(self, T=2048, D=1024, H=16, HID=2048, ER=8, ES=2,
                 TCS=512, TC2=256, HGSZ=2):
        self.T, self.D, self.H, self.HID = T, D, H, HID
        self.ER, self.ES = ER, ES
        self.DH = 64
        self.TB = T // 128
        self.DC = D // 128
        self.PAIRS = H // 2
        self.HC = HID // 128
        self.TCS = min(TCS, T)
        self.NQC = T // self.TCS
        self.TC2 = min(TC2, T)
        self.TB2 = self.TC2 // 128
        self.N2 = min(512, D)
        self.ND2 = D // self.N2
        self.HGSZ = HGSZ
        self.NHG = self.HC // HGSZ
        self.NE = ES + ER
        assert T % 128 == 0 and D % 128 == 0 and HID % 128 == 0
        assert self.TB2 * self.ND2 <= 4
        assert self.ER == 8, "router uses max8 (needs exactly >=8 logits)"


FULL = Cfg()


class Sem:
    def __init__(self, h):
        self.h = h
        self.n = 0

    def inc(self, k=1):
        self.n += k
        return self.n

    def w(self):
        return (self, self.n)


class Prog:

    def __init__(self):
        self.ops = []

    def op(self, eng, fn, waits=(), inc=None):
        waits = [(s, v) for (s, v) in waits if v > 0]
        if inc is not None:
            sem, k = inc
            sem.inc(k)
        self.ops.append((eng, waits, fn, inc))

    def barrier(self, eng, waits):
        self.op(eng, None, waits=waits)

    def replay(self, block, engines):
        streams = {name: [] for name in engines}
        for (eng, waits, fn, inc) in self.ops:
            streams[eng].append((waits, fn, inc))

        def make_body(name):
            def body(e):
                last_wait = {}
                for (waits, fn, inc) in streams[name]:
                    for (sem, val) in waits:
                        if last_wait.get(id(sem), -1) >= val:
                            continue
                        e.wait_ge(sem.h, val)
                        last_wait[id(sem)] = val
                    if fn is None:
                        assert inc is None
                        continue
                    inst = fn(e)
                    if inc is not None:
                        sem, k = inc
                        inst.then_inc(sem.h, k)
            return body
        for name in engines:
            getattr(block, name)(make_body(name))



def _finish(nc, P, sems):
    nums = sorted(s.h.num for s in sems.values())
    assert nums == list(range(nums[0], nums[0] + len(nums))), nums
    rng = range(nums[0], nums[-1] + 1)
    with nc.Block() as block:
        nc.all_engine_barrier()
        nc.gpsimd.dma_reset(rng)
        nc.gpsimd.sem_clear(rng)
        nc.all_engine_barrier()
        P.replay(block, ["sync", "gpsimd", "scalar", "vector", "tensor"])


def build_nc(cfg: Cfg, phases="ABCDEF", debug_dump=False):
    c = cfg
    nc = bass.Bass("TRN2", target_bir_lowering=False,
                   detect_race_conditions=False, dynamic_dma_scratch_size=4096)

    dx0 = nc.dram_tensor("x0", [c.T, c.D], F32, kind="ExternalInput").ap()
    dwk = nc.dram_tensor("wk", [128, c.PAIRS * c.D], F32, kind="ExternalInput").ap()
    dwq = nc.dram_tensor("wq", [128, c.PAIRS * c.D], F32, kind="ExternalInput").ap()
    dwv = nc.dram_tensor("wv", [128, c.PAIRS * c.D], F32, kind="ExternalInput").ap()
    dwo = nc.dram_tensor("wo", [128, c.PAIRS * c.D], F32, kind="ExternalInput").ap()
    dbk = nc.dram_tensor("bk", [128, c.PAIRS], F32, kind="ExternalInput").ap()
    dbq = nc.dram_tensor("bq", [128, c.PAIRS], F32, kind="ExternalInput").ap()
    dbv = nc.dram_tensor("bv", [128, c.PAIRS], F32, kind="ExternalInput").ap()
    dbo = nc.dram_tensor("bo", [1, c.D], F32, kind="ExternalInput").ap()
    dl1w = nc.dram_tensor("ln1w", [128, c.DC], F32, kind="ExternalInput").ap()
    dl1b = nc.dram_tensor("ln1b", [128, c.DC], F32, kind="ExternalInput").ap()
    dl2w = nc.dram_tensor("ln2w", [128, c.D], F32, kind="ExternalInput").ap()
    dl2b = nc.dram_tensor("ln2b", [128, c.D], F32, kind="ExternalInput").ap()
    drtw = nc.dram_tensor("rtw", [128, c.DC * c.ER], F32, kind="ExternalInput").ap()
    drtb = nc.dram_tensor("rtb", [128, c.ER], F32, kind="ExternalInput").ap()
    dw1 = nc.dram_tensor("w1s", [c.NE, 128, c.HC * c.D], F32, kind="ExternalInput").ap()
    dw3 = nc.dram_tensor("w3s", [c.NE, 128, c.HC * c.D], F32, kind="ExternalInput").ap()
    dw2 = nc.dram_tensor("w2s", [c.NE, 128, c.HC * c.D], F32, kind="ExternalInput").ap()
    dy = nc.dram_tensor("y", [c.T, c.D], F32, kind="ExternalOutput").ap()

    A = lambda name, shape, dt=F32: nc.alloc_sbuf_tensor(name, shape, dt).ap()
    BF16 = mybir.dt.bfloat16
    xfmP = nc.alloc_sbuf_tensor("xfmP", [128, 2 * c.DC * c.T], BF16).ap()
    xfm_h = xfmP[:, 0:c.DC * c.T]
    xfm_l = xfmP[:, c.DC * c.T:2 * c.DC * c.T]
    big2 = A("big2", [128, c.TB * c.D])
    sb_len = max(3 * c.T + c.TB * 130, c.HGSZ * c.T)
    sbig = A("sbig", [128, sb_len])
    kpair = sbig[:, 0:c.T]
    qpair = sbig[:, c.T:2 * c.T]
    zpair = sbig[:, 2 * c.T:3 * c.T]
    vt_off = 3 * c.T
    WBW = 12 * c.D + c.D // 2
    wbuf = nc.alloc_sbuf_tensor("wbufB", [128, WBW], BF16).ap()

    def wpair(i):
        return (wbuf[:, 2 * i * c.D:(2 * i + 1) * c.D],
                wbuf[:, (2 * i + 1) * c.D:(2 * i + 2) * c.D])
    wo_f32 = wbuf[:, 6 * c.D:8 * c.D].bitcast(F32)
    LQ = c.D // 8
    land = [wbuf[:, 12 * c.D + 2 * i * LQ:12 * c.D + 2 * (i + 1) * LQ].bitcast(F32)
            for i in range(2)]
    NJ = c.TCS // 128
    WCW = NJ * 128 + c.TCS
    ovlrp = A("ovlrp", [128, WCW + max(c.D, 4 * c.TCS)])
    cptW = ovlrp[:, 0:WCW]
    cpt = cptW[:, 0:c.TCS]
    ptile = ovlrp[:, WCW:WCW + 4 * c.TCS]
    resid = ovlrp[:, WCW:WCW + c.D]
    scratch = A("scratch", [128, max(2 * c.D, c.D + c.TCS + 130)])
    l2w = A("l2wr", [128, c.D])
    l2b = A("l2br", [128, c.D])
    ident = A("ident", [128, 128])
    onesr = A("onesr", [1, 128])
    bkt = A("bkt", [128, c.PAIRS])
    bqt = A("bqt", [128, c.PAIRS])
    bvc = A("bvc", [128, c.PAIRS])
    bot = A("bot", [1, c.D])
    l1w = A("l1w", [128, c.DC])
    l1b = A("l1b", [128, c.DC])
    rtwP = nc.alloc_sbuf_tensor("rtwP", [128, 2 * c.DC * c.ER],
                                mybir.dt.bfloat16).ap()
    rtb = A("rtbr", [128, c.ER])
    iot8 = A("iot8", [128, c.ER])
    gates = A("gates", [128, c.TB * c.ER])
    m8 = A("m8", [128, 8])
    i8 = A("i8", [128, 8], U32)
    fi2 = A("fi2", [128, 2])
    sc1 = A("sc1", [128, 1])
    sc2 = A("sc2", [128, 1])
    w1g = A("w1g", [128, 1])
    w2g = A("w2g", [128, 1])
    bn12 = A("bn12", [128, 12])
    mv2 = A("mv2", [128, 2])
    rstd = A("rstd", [128, 1])
    negm = A("negm", [128, 1])
    logt = A("logt", [128, c.ER])
    cneg8 = A("cneg8", [128, 1])
    dstrip = A("dstrip", [128, c.TB * 16]) if debug_dump else None
    rden = scratch[0:1, 0:2 * c.TCS]
    sbc = scratch[:, c.D:c.D + c.TCS]
    vsuf = scratch[:, c.D + c.TCS:c.D + c.TCS + 130]

    ps = [nc.alloc_psum_tensor(f"ps{i}", [128, 512], F32).ap() for i in range(8)]

    sems = {}
    for name in ["w", "x", "o", "t", "s", "v", "g"]:
        sems[name] = Sem(nc.semaphore(f"sem_{name}").__enter__())
    sw, sx, so, st_, ss, sv, sg = (sems[k] for k in ["w", "x", "o", "t", "s", "v", "g"])

    P = Prog()

    land_free = [0, 0]
    land_rr = [0]

    def stream_pair(src_ap, W, dst_h, dst_l, pre_wait=None):
        nq = W // LQ
        first = True
        for q in range(nq):
            sl = land_rr[0] % 2
            land_rr[0] += 1
            P.op("sync", lambda e, q=q, sl=sl, s=src_ap: e.dma_start(
                out=land[sl], in_=s[:, q * LQ:(q + 1) * LQ]),
                waits=[(sg, land_free[sl])], inc=(sw, 16))
            wv_ = sw.n
            waits = [(sw, wv_)]
            if first and pre_wait is not None:
                waits.append(pre_wait)
                first = False
            P.op("vector", lambda e, q=q, sl=sl, d=dst_h: e.tensor_copy(
                d[:, q * LQ:(q + 1) * LQ], land[sl]), waits=waits, inc=(sv, 1))
            hw_ = sv.n
            P.op("gpsimd", lambda e, q=q, sl=sl, dh=dst_h, dl=dst_l: e.tensor_tensor(
                dl[:, q * LQ:(q + 1) * LQ], land[sl],
                dh[:, q * LQ:(q + 1) * LQ], op=ALU.subtract),
                waits=[(sv, hw_)], inc=(sg, 1))
            land_free[sl] = sg.n
        return sg.n

    def mm3(out_ap, lh, ll, rh, rl, start, stop, waits=(), inc=None):
        P.op("tensor", lambda e, o=out_ap, a=lh, b=rh, s=start: e.matmul(
            o, a, b, start=s, stop=False), waits=waits)
        P.op("tensor", lambda e, o=out_ap, a=ll, b=rh: e.matmul(
            o, a, b, start=False, stop=False))
        P.op("tensor", lambda e, o=out_ap, a=lh, b=rl, s=stop: e.matmul(
            o, a, b, start=False, stop=s), inc=inc)

    def vtok(tb, half):
        off = vt_off + tb * 130 + half * 65
        return sbig[:, off:off + 65]

    for (dst, src) in [(bkt, dbk), (bqt, dbq), (bvc, dbv), (bot, dbo),
                       (l1w, dl1w), (l1b, dl1b), (l2w, dl2w), (l2b, dl2b),
                       (rtb, drtb)]:
        P.op("sync", lambda e, d=dst, s=src: e.dma_start(out=d, in_=s), inc=(sw, 16))
    n_small = 9

    P.op("gpsimd", lambda e: e.memset(ident, 1.0))
    P.op("gpsimd", lambda e: e.affine_select(
        ident, ident, pattern=[[1, 128]], compare_op=ALU.is_equal,
        fill=0.0, base=0, channel_multiplier=-1), inc=(sg, 1))
    P.op("gpsimd", lambda e: e.memset(onesr, 1.0))
    P.op("gpsimd", lambda e: e.memset(cptW, -CSHIFT), inc=(sg, 1))
    P.op("scalar", lambda e: e.activation(cptW, cptW, AF.Exp),
         waits=[sg.w()], inc=(ss, 1))
    P.op("gpsimd", lambda e: e.affine_select(
        cptW, cptW, pattern=[[-1, WCW]], compare_op=ALU.is_ge,
        fill=0.0, base=NJ * 128 - 1, channel_multiplier=1),
        waits=[ss.w()])
    P.op("gpsimd", lambda e: e.iota(iot8, pattern=[[1, c.ER]], channel_multiplier=0, allow_small_or_imprecise_dtypes=True), inc=(sg, 1))
    P.op("gpsimd", lambda e: e.memset(cneg8, -CSHIFT))
    NRW = c.DC * c.ER
    P.op("sync", lambda e: e.dma_start(out=land[0][:, 0:NRW], in_=drtw),
         inc=(sw, 16))
    rtw_w = sw.n
    P.op("gpsimd", lambda e: e.tensor_copy(rtwP[:, 0:NRW], land[0][:, 0:NRW]),
         waits=[(sw, rtw_w)])
    P.op("gpsimd", lambda e: e.tensor_tensor(
        rtwP[:, NRW:2 * NRW], land[0][:, 0:NRW], rtwP[:, 0:NRW],
        op=ALU.subtract), inc=(sg, 1))
    land_free[0] = sg.n
    P.op("gpsimd", lambda e: e.memset(
        sbig[:, vt_off:vt_off + c.TB * 130].rearrange(
            "p (t o) -> p t o", o=130)[:, :, 64:65], 1.0))
    P.op("gpsimd", lambda e: e.memset(
        sbig[:, vt_off:vt_off + c.TB * 130].rearrange(
            "p (t o) -> p t o", o=130)[:, :, 129:130], 1.0), inc=(sg, 1))
    setup_g = sg.n

    def ln_stats(src_ap, width, waits):
        nchunk = (width + 511) // 512
        assert nchunk in (1, 2)
        for i in range(nchunk):
            w0 = i * 512
            w1 = min(width, w0 + 512)
            P.op("vector", lambda e, a=src_ap, w0=w0, w1=w1, i=i: e.bn_stats(
                bn12[:, 6 * i:6 * i + 6], a[:, w0:w1]), waits=waits)
        P.op("vector", lambda e: e.drain())
        P.op("vector", lambda e, n=nchunk: e.bn_aggr(mv2, bn12[:, 0:6 * n]))
        P.op("vector", lambda e: e.drain())
        P.op("vector", lambda e: e.tensor_scalar(
            sc1, mv2[:, 1:2], LN_EPS, None, op0=ALU.add), inc=(sv, 1))
        P.op("scalar", lambda e: e.activation(sc2, sc1, AF.Sqrt),
             waits=[sv.w()], inc=(ss, 1))
        P.op("vector", lambda e: e.reciprocal(rstd, sc2), waits=[ss.w()])
        P.op("vector", lambda e: e.drain())
        P.op("vector", lambda e: e.tensor_scalar(
            negm, mv2[:, 0:1], rstd, -1.0, op0=ALU.mult, op1=ALU.mult))
        P.op("vector", lambda e: e.drain(), inc=(sv, 1))

    def ln_apply(dst_ap, src_ap, waits):
        P.op("vector", lambda e: e.tensor_scalar(
            dst_ap, src_ap, mv2[:, 0:1], rstd, op0=ALU.subtract, op1=ALU.mult),
            waits=waits, inc=(sv, 1))

    xn_a = scratch[:, 0:c.D]
    tmp_free = [0, 0]
    for tb in range(c.TB):
        P.op("scalar", lambda e, tb=tb: e.dma_start(
            out=resid, in_=dx0[tb * 128:(tb + 1) * 128, :]),
            waits=[sv.w()], inc=(sx, 16))
        ln_stats(resid, c.D, waits=[(sx, sx.n)])
        ln_apply(xn_a, resid, waits=[])
        for dc in range(c.DC):
            bank = ps[6 + (dc % 2)]
            P.op("tensor", lambda e, b=bank, dc=dc: e.transpose(
                b[:, 0:128], xn_a[:, dc * 128:(dc + 1) * 128], ident),
                waits=[(sv, sv.n), (sg, setup_g)], inc=(st_, 1))
            col = dc * c.T + tb * 128
            tmp = scratch[:, c.D + (dc % 2) * 128:c.D + (dc % 2) * 128 + 128]
            P.op("vector", lambda e, b=bank, t=tmp, dc=dc: e.tensor_scalar(
                t, b[:, 0:128],
                l1w[:, dc:dc + 1], l1b[:, dc:dc + 1], op0=ALU.mult, op1=ALU.add),
                waits=[st_.w(), (sw, n_small * 16), (sg, tmp_free[dc % 2])],
                inc=(sv, 1))
            P.op("gpsimd", lambda e, t=tmp, col=col: e.tensor_copy(
                xfm_h[:, col:col + 128], t), waits=[sv.w()])
            P.op("gpsimd", lambda e, t=tmp, col=col: e.tensor_tensor(
                xfm_l[:, col:col + 128], t, xfm_h[:, col:col + 128],
                op=ALU.subtract), inc=(sg, 1))
            tmp_free[dc % 2] = sg.n

    if "B" not in phases:
        if debug_dump:
            dxf = nc.dram_tensor("dbg_xfm", [128, c.DC * c.T], F32, kind="ExternalOutput").ap()
            dbg = nc.dram_tensor("dbg_big2", [128, c.TB * c.D], F32, kind="ExternalOutput").ap()
            dsb = nc.dram_tensor("dbg_sbig", [128, sb_len], F32, kind="ExternalOutput").ap()
            dsc = nc.dram_tensor("dbg_scr", [128, 2 * c.D], F32, kind="ExternalOutput").ap()
            P.op("sync", lambda e: e.dma_start(out=dxf, in_=xfm),
                 waits=[(sv, sv.n), (ss, ss.n), (st_, st_.n), (sg, sg.n), (sw, sw.n)],
                 inc=(so, 16))
            P.op("sync", lambda e: e.dma_start(out=dbg, in_=big2), inc=(so, 16))
            P.op("sync", lambda e: e.dma_start(out=dsb, in_=sbig), inc=(so, 16))
            P.op("sync", lambda e: e.dma_start(out=dsc, in_=scratch), inc=(so, 16))
            dgt = nc.dram_tensor("dbg_gates", [128, c.TB * c.ER], F32, kind="ExternalOutput").ap()
            P.op("sync", lambda e: e.dma_start(out=dgt, in_=gates), inc=(so, 16))
            dst2 = nc.dram_tensor("dbg_strip", [128, c.TB * 16], F32, kind="ExternalOutput").ap()
            P.op("sync", lambda e: e.dma_start(out=dst2, in_=dstrip), inc=(so, 16))
            P.barrier("sync", waits=[(so, so.n)])

        _finish(nc, P, sems)
        return nc

    wkP, wqP, wvP = wpair(0), wpair(1), wpair(2)
    last_kq_use = 0
    last_wv_use = 0
    last_wo_use = 0
    NKB = c.T // 128

    for p in range(c.PAIRS):
        kq_rdy = stream_pair(dwk[:, p * c.D:(p + 1) * c.D], c.D, wkP[0], wkP[1],
                             pre_wait=(st_, last_kq_use))
        kq_rdy = stream_pair(dwq[:, p * c.D:(p + 1) * c.D], c.D, wqP[0], wqP[1],
                             pre_wait=(st_, last_kq_use))
        wv_rdy = stream_pair(dwv[:, p * c.D:(p + 1) * c.D], c.D, wvP[0], wvP[1],
                             pre_wait=(st_, last_wv_use))
        P.op("sync", lambda e, p=p: e.dma_start(
            out=wo_f32, in_=dwo[:, p * c.D:(p + 1) * c.D]),
            waits=[(st_, last_wo_use)], inc=(sw, 16))
        w_ready = sw.n

        for (wsrc, dst, bias) in [(wkP, kpair, bkt), (wqP, qpair, bqt)]:
            for tc in range(c.NQC):
                bank = ps[6 + (tc % 2)]
                for dc in range(c.DC):
                    xo = dc * c.T + tc * c.TCS
                    mm3(bank[:, 0:c.TCS],
                        wsrc[0][:, dc * 128:(dc + 1) * 128],
                        wsrc[1][:, dc * 128:(dc + 1) * 128],
                        xfm_h[:, xo:xo + c.TCS], xfm_l[:, xo:xo + c.TCS],
                        start=(dc == 0), stop=(dc == c.DC - 1),
                        waits=([(sg, kq_rdy), (sv, sv.n), (ss, ss.n)]
                               if dc == 0 else []),
                        inc=(st_, 1) if dc == c.DC - 1 else None)
                P.op("scalar", lambda e, b=bank, d=dst, tc=tc, bias=bias, p=p:
                     e.activation(d[:, tc * c.TCS:(tc + 1) * c.TCS],
                                  b[:, 0:c.TCS], AF.Identity,
                                  bias=bias[:, p:p + 1], scale=1.0),
                     waits=[st_.w()], inc=(ss, 1))
        last_kq_use = st_.n

        for tb in range(c.TB):
            bank = ps[6 + (tb % 2)]
            for dc in range(c.DC):
                xo = dc * c.T + tb * 128
                mm3(bank[:, 0:128],
                    xfm_h[:, xo:xo + 128], xfm_l[:, xo:xo + 128],
                    wvP[0][:, dc * 128:(dc + 1) * 128],
                    wvP[1][:, dc * 128:(dc + 1) * 128],
                    start=(dc == 0), stop=(dc == c.DC - 1),
                    waits=[(ss, ss.n), (sg, wv_rdy)] if dc == 0 else [],
                    inc=(st_, 1) if dc == c.DC - 1 else None)
            off = vt_off + tb * 130
            P.op("scalar", lambda e, b=bank, off=off: e.copy(
                sbig[:, off:off + 64], b[:, 0:64]),
                waits=[st_.w(), (sg, setup_g)], inc=(ss, 1))
            P.op("scalar", lambda e, b=bank, off=off: e.copy(
                sbig[:, off + 65:off + 129], b[:, 64:128]), inc=(ss, 1))
        last_wv_use = st_.n

        zevac_v = sv.n
        qk_ready = ss.n
        bank_free = {}
        bc_free = {0: 0, 1: 0}
        suf_start = NKB
        vsuf_v = 0
        for qc in range(c.NQC - 1, -1, -1):
            nkb_act = min(NKB, (qc + 1) * c.TCS // 128)
            if nkb_act < suf_start:
                for kb in range(suf_start - 1, nkb_act - 1, -1):
                    for h in range(2):
                        vs = vsuf[:, h * 65:(h + 1) * 65]
                        if suf_start == NKB and kb == suf_start - 1:
                            pass
                        if kb == NKB - 1:
                            P.op("vector", lambda e, vs=vs, kb=kb, h=h: e.tensor_copy(
                                vs, vtok(kb, h)), waits=[(ss, qk_ready)])
                        else:
                            P.op("vector", lambda e, vs=vs, kb=kb, h=h: e.tensor_tensor(
                                vs, vtok(kb, h), vs, op=ALU.add),
                                waits=[(ss, qk_ready)])
                suf_start = nkb_act
                P.op("vector", lambda e: e.drain(), inc=(sv, 1))
                vsuf_v = sv.n
            zb = [ps[4], ps[5]]
            exp_done = {}

            def emit_S(kb, h, qc=qc):
                sb = ps[2 * h + (kb % 2)]
                lo, hi = 64 * h, 64 * h + 64
                waits = [(ss, qk_ready)]
                if (h, kb % 2) in bank_free:
                    waits.append((ss, bank_free[(h, kb % 2)]))
                if kb % 2 == 0 and bc_free[h]:
                    waits.append((ss, bc_free[h]))
                P.op("tensor",
                     lambda e, b=sb, kb=kb, qc=qc, lo=lo, hi=hi: e.matmul(
                         b[:, 0:c.TCS],
                         kpair[lo:hi, kb * 128:(kb + 1) * 128],
                         qpair[lo:hi, qc * c.TCS:(qc + 1) * c.TCS],
                         start=True, stop=True),
                     waits=waits, inc=(st_, 1))
                s_done = st_.n
                pt = ptile[:, (2 * h + kb % 2) * c.TCS:
                           (2 * h + kb % 2 + 1) * c.TCS]
                base = qc * c.TCS - kb * 128
                if base < 127:
                    j = kb - qc * NJ
                    assert 0 <= j < NJ and base == -128 * j
                    if j > 0:
                        P.op("gpsimd", lambda e, pt=pt, j=j: e.tensor_copy(
                            pt[:, 0:128 * j], cpt[:, 0:128 * j]),
                            waits=[(st_, s_done)])
                    P.op("scalar", lambda e, b=sb, pt=pt, j=j: e.activation(
                        pt[:, 128 * j:c.TCS], b[:, 128 * j:c.TCS],
                        AF.Exp, bias=cneg8, scale=0.125),
                        waits=[(st_, s_done)], inc=(ss, 1))
                    bank_free[(h, kb % 2)] = ss.n
                    P.op("gpsimd", lambda e, pt=pt, j=j: e.affine_select(
                        pt[:, 128 * j:128 * j + 128],
                        pt[:, 128 * j:128 * j + 128],
                        pattern=[[1, 128]], compare_op=ALU.is_ge,
                        fill=0.0, base=0, channel_multiplier=-1),
                        waits=[ss.w()])
                    cwn = cptW[:, NJ * 128:NJ * 128 + 128]
                    P.op("gpsimd", lambda e, pt=pt, j=j, cwn=cwn: e.tensor_tensor(
                        pt[:, 128 * j:128 * j + 128],
                        pt[:, 128 * j:128 * j + 128], cwn, op=ALU.add),
                        inc=(sg, 1))
                    exp_done[(kb, h)] = ("g", sg.n, pt)
                else:
                    P.op("scalar", lambda e, b=sb, pt=pt: e.activation(
                        pt, b[:, 0:c.TCS], AF.Exp, bias=cneg8, scale=0.125),
                        waits=[(st_, s_done)], inc=(ss, 1))
                    bank_free[(h, kb % 2)] = ss.n
                    exp_done[(kb, h)] = ("s", ss.n, pt)

            last_active_stops = (nkb_act == NKB)

            def emit_z(kb, h):
                who, val, pt = exp_done[(kb, h)]
                waits = [(sg, val) if who == "g" else (ss, val)]
                if kb == 0:
                    waits.append((sv, zevac_v))
                stop = last_active_stops and kb == NKB - 1
                P.op("tensor", lambda e, zb=zb[h], kb=kb, h=h, pt=pt, stop=stop:
                     e.matmul(zb[0:65, 0:c.TCS], vtok(kb, h), pt,
                              start=(kb == 0), stop=stop),
                     waits=waits, inc=(st_, 1) if stop else None)

            for kb in range(nkb_act):
                emit_S(kb, 0)
                emit_S(kb, 1)
                if kb > 0:
                    emit_z(kb - 1, 0)
                    emit_z(kb - 1, 1)
            emit_z(nkb_act - 1, 0)
            emit_z(nkb_act - 1, 1)
            if not last_active_stops:
                for h in range(2):
                    P.op("tensor", lambda e, zb=zb[h], h=h: e.matmul(
                        zb[0:65, 0:c.TCS], vsuf[:, h * 65:(h + 1) * 65], cpt,
                        start=False, stop=True),
                        waits=[(sv, vsuf_v)], inc=(st_, 1))
            z_done = st_.n
            for h in range(2):
                P.op("vector", lambda e, h=h, zb=zb[h]: e.reciprocal(
                    rden[0:1, h * c.TCS:(h + 1) * c.TCS],
                    zb[64:65, 0:c.TCS]), waits=[(st_, z_done)], inc=(sv, 1))
            recip_v = sv.n
            for h in range(2):
                P.op("tensor", lambda e, h=h: e.matmul(
                    ps[2 * h][0:64, 0:c.TCS], onesr[0:1, 0:64],
                    rden[0:1, h * c.TCS:(h + 1) * c.TCS],
                    start=True, stop=True), waits=[(sv, recip_v)], inc=(st_, 1))
            bc_done = st_.n
            for h in range(2):
                P.op("scalar", lambda e, h=h: e.copy(
                    sbc[64 * h:64 * h + 64, 0:c.TCS],
                    ps[2 * h][0:64, 0:c.TCS]), waits=[(st_, bc_done)], inc=(ss, 1))
                bc_free[h] = ss.n
            sbc_done = ss.n
            for h in range(2):
                P.op("vector", lambda e, h=h, zb=zb[h], qc=qc: e.tensor_tensor(
                    zpair[64 * h:64 * h + 64, qc * c.TCS:(qc + 1) * c.TCS],
                    zb[0:64, 0:c.TCS], sbc[64 * h:64 * h + 64, 0:c.TCS],
                    op=ALU.mult), waits=[(ss, sbc_done)])
            P.op("vector", lambda e: e.drain())
            for h in range(2):
                P.op("vector", lambda e, h=h, qc=qc, p=p: e.tensor_scalar(
                    zpair[64 * h:64 * h + 64, qc * c.TCS:(qc + 1) * c.TCS],
                    zpair[64 * h:64 * h + 64, qc * c.TCS:(qc + 1) * c.TCS],
                    bvc[64 * h:64 * h + 64, p:p + 1], None, op0=ALU.add),
                    inc=(sv, 1))
            zevac_v = sv.n

        for tb in range(c.TB):
            for dcc in range(c.D // 512):
                bank = ps[(tb * (c.D // 512) + dcc) % 4]
                P.op("tensor", lambda e, b=bank, tb=tb, dcc=dcc, p=p: e.matmul(
                    b[:, 0:512], zpair[:, tb * 128:(tb + 1) * 128],
                    wo_f32[:, dcc * 512:(dcc + 1) * 512],
                    start=True, stop=(p != 0)),
                    waits=[(sv, sv.n)],
                    inc=None if p == 0 else (st_, 1))
                if p == 0:
                    P.op("tensor", lambda e, b=bank, dcc=dcc: e.matmul(
                        b[:, 0:512], onesr, bot[0:1, dcc * 512:(dcc + 1) * 512],
                        start=False, stop=True), inc=(st_, 1))
                col = tb * c.D + dcc * 512
                if p == 0:
                    P.op("vector", lambda e, b=bank, col=col: e.tensor_copy(
                        big2[:, col:col + 512], b[:, 0:512]),
                        waits=[st_.w()], inc=(sv, 1))
                else:
                    P.op("vector", lambda e, b=bank, col=col: e.tensor_tensor(
                        big2[:, col:col + 512], b[:, 0:512],
                        big2[:, col:col + 512], op=ALU.add),
                        waits=[st_.w()], inc=(sv, 1))
        last_wo_use = st_.n

    if "C" not in phases:
        if debug_dump:
            dxf = nc.dram_tensor("dbg_xfm", [128, c.DC * c.T], F32, kind="ExternalOutput").ap()
            dbg = nc.dram_tensor("dbg_big2", [128, c.TB * c.D], F32, kind="ExternalOutput").ap()
            dsb = nc.dram_tensor("dbg_sbig", [128, sb_len], F32, kind="ExternalOutput").ap()
            dsc = nc.dram_tensor("dbg_scr", [128, 2 * c.D], F32, kind="ExternalOutput").ap()
            P.op("sync", lambda e: e.dma_start(out=dxf, in_=xfm),
                 waits=[(sv, sv.n), (ss, ss.n), (st_, st_.n), (sg, sg.n), (sw, sw.n)],
                 inc=(so, 16))
            P.op("sync", lambda e: e.dma_start(out=dbg, in_=big2), inc=(so, 16))
            P.op("sync", lambda e: e.dma_start(out=dsb, in_=sbig), inc=(so, 16))
            P.op("sync", lambda e: e.dma_start(out=dsc, in_=scratch), inc=(so, 16))
            dgt = nc.dram_tensor("dbg_gates", [128, c.TB * c.ER], F32, kind="ExternalOutput").ap()
            P.op("sync", lambda e: e.dma_start(out=dgt, in_=gates), inc=(so, 16))
            dst2 = nc.dram_tensor("dbg_strip", [128, c.TB * 16], F32, kind="ExternalOutput").ap()
            P.op("sync", lambda e: e.dma_start(out=dst2, in_=dstrip), inc=(so, 16))
            P.barrier("sync", waits=[(so, so.n)])

        _finish(nc, P, sems)
        return nc
    rm = scratch[:, 0:c.D]
    xn = scratch[:, c.D:2 * c.D]
    xfm_tb_v = {}
    DH2 = c.D // 2

    def split_tt(dst, a, b_, op):
        P.op("vector", lambda e: e.tensor_tensor(
            dst[:, 0:DH2], a[:, 0:DH2], b_[:, 0:DH2], op=op))
        P.op("gpsimd", lambda e: e.tensor_tensor(
            dst[:, DH2:c.D], a[:, DH2:c.D], b_[:, DH2:c.D], op=op),
            waits=[(sv, sv.n)])

    for tb in range(c.TB):
        P.op("scalar", lambda e, tb=tb: e.dma_start(
            out=resid, in_=dx0[tb * 128:(tb + 1) * 128, :]),
            waits=[(sv, sv.n), (sg, sg.n)], inc=(sx, 16))
        acc_tb = big2[:, tb * c.D:(tb + 1) * c.D]
        P.op("vector", lambda e, a=acc_tb: e.tensor_tensor(
            rm[:, 0:DH2], a[:, 0:DH2], resid[:, 0:DH2], op=ALU.add),
            waits=[(sx, sx.n), (sv, sv.n)], inc=(sv, 1))
        P.op("gpsimd", lambda e, a=acc_tb: e.tensor_tensor(
            rm[:, DH2:c.D], a[:, DH2:c.D], resid[:, DH2:c.D], op=ALU.add),
            waits=[(sx, sx.n), (sv, sv.n)], inc=(sg, 1))
        ln_stats(rm, c.D, waits=[(sg, sg.n)])
        ln_apply(xn, rm, waits=[])
        gw = sg.n
        P.op("gpsimd", lambda e: e.tensor_scalar(
            xn[:, DH2:c.D], rm[:, DH2:c.D], mv2[:, 0:1], rstd,
            op0=ALU.subtract, op1=ALU.mult), waits=[(sv, sv.n)], inc=(sg, 1))
        P.op("vector", lambda e: e.tensor_tensor(
            xn[:, 0:DH2], l2w[:, 0:DH2], xn[:, 0:DH2], op=ALU.mult))
        P.op("gpsimd", lambda e: e.tensor_tensor(
            xn[:, DH2:c.D], l2w[:, DH2:c.D], xn[:, DH2:c.D], op=ALU.mult))
        P.op("vector", lambda e: e.tensor_tensor(
            xn[:, 0:DH2], l2b[:, 0:DH2], xn[:, 0:DH2], op=ALU.add))
        P.op("gpsimd", lambda e: e.tensor_tensor(
            xn[:, DH2:c.D], l2b[:, DH2:c.D], xn[:, DH2:c.D], op=ALU.add))
        P.op("vector", lambda e: e.tensor_tensor(
            rm[:, 0:DH2], xn[:, 0:DH2], rm[:, 0:DH2], op=ALU.add))
        P.op("gpsimd", lambda e: e.tensor_tensor(
            rm[:, DH2:c.D], xn[:, DH2:c.D], rm[:, DH2:c.D], op=ALU.add))
        P.op("vector", lambda e: e.tensor_tensor(
            rm[:, 0:DH2], resid[:, 0:DH2], rm[:, 0:DH2], op=ALU.add),
            inc=(sv, 1))
        P.op("gpsimd", lambda e: e.tensor_tensor(
            rm[:, DH2:c.D], resid[:, DH2:c.D], rm[:, DH2:c.D], op=ALU.add),
            inc=(sg, 1))
        ln_stats(rm, c.D, waits=[(sg, sg.n)])
        ln_apply(big2[:, tb * c.D:(tb + 1) * c.D], rm, waits=[])
        xa_v = sv.n
        for dc in range(c.DC):
            bank = ps[6 + (dc % 2)]
            P.op("tensor", lambda e, b=bank, tb=tb, dc=dc: e.transpose(
                b[:, 0:128], big2[:, tb * c.D + dc * 128: tb * c.D + dc * 128 + 128],
                ident), waits=[(sv, xa_v), (sv, sv.n)], inc=(st_, 1))
            col = dc * c.T + tb * 128
            tmp = scratch[:, c.D + (dc % 2) * 128:c.D + (dc % 2) * 128 + 128]
            P.op("vector", lambda e, b=bank, t=tmp: e.tensor_copy(
                t, b[:, 0:128]), waits=[st_.w(), (sg, tmp_free[dc % 2])],
                inc=(sv, 1))
            P.op("gpsimd", lambda e, t=tmp, col=col: e.tensor_copy(
                xfm_h[:, col:col + 128], t), waits=[sv.w()])
            P.op("gpsimd", lambda e, t=tmp, col=col: e.tensor_tensor(
                xfm_l[:, col:col + 128], t, xfm_h[:, col:col + 128],
                op=ALU.subtract), inc=(sg, 1))
            tmp_free[dc % 2] = sg.n
        xfm_tb_v[tb] = sg.n

    if "D" not in phases:
        if debug_dump:
            dxf = nc.dram_tensor("dbg_xfm", [128, c.DC * c.T], F32, kind="ExternalOutput").ap()
            dbg = nc.dram_tensor("dbg_big2", [128, c.TB * c.D], F32, kind="ExternalOutput").ap()
            dsb = nc.dram_tensor("dbg_sbig", [128, sb_len], F32, kind="ExternalOutput").ap()
            dsc = nc.dram_tensor("dbg_scr", [128, 2 * c.D], F32, kind="ExternalOutput").ap()
            P.op("sync", lambda e: e.dma_start(out=dxf, in_=xfm),
                 waits=[(sv, sv.n), (ss, ss.n), (st_, st_.n), (sg, sg.n), (sw, sw.n)],
                 inc=(so, 16))
            P.op("sync", lambda e: e.dma_start(out=dbg, in_=big2), inc=(so, 16))
            P.op("sync", lambda e: e.dma_start(out=dsb, in_=sbig), inc=(so, 16))
            P.op("sync", lambda e: e.dma_start(out=dsc, in_=scratch), inc=(so, 16))
            dgt = nc.dram_tensor("dbg_gates", [128, c.TB * c.ER], F32, kind="ExternalOutput").ap()
            P.op("sync", lambda e: e.dma_start(out=dgt, in_=gates), inc=(so, 16))
            dst2 = nc.dram_tensor("dbg_strip", [128, c.TB * 16], F32, kind="ExternalOutput").ap()
            P.op("sync", lambda e: e.dma_start(out=dst2, in_=dstrip), inc=(so, 16))
            P.barrier("sync", waits=[(so, so.n)])

        _finish(nc, P, sems)
        return nc
    def emit_router():
        logt_free = ss.n
        for tb in range(c.TB):
            for dc in range(c.DC):
                xo = dc * c.T + tb * 128
                mm3(ps[6][:, 0:c.ER],
                    xfm_h[:, xo:xo + 128], xfm_l[:, xo:xo + 128],
                    rtwP[:, dc * c.ER:(dc + 1) * c.ER],
                    rtwP[:, NRW + dc * c.ER:NRW + (dc + 1) * c.ER],
                    start=(dc == 0), stop=(dc == c.DC - 1),
                    waits=([(sv, sv.n), (sg, sg.n), (ss, logt_free)]
                           if dc == 0 else []),
                    inc=(st_, 1) if dc == c.DC - 1 else None)
            P.op("scalar", lambda e: e.copy(logt, ps[6][:, 0:c.ER]),
                 waits=[st_.w()], inc=(ss, 1))
            logt_free = ss.n
            P.op("vector", lambda e: e.tensor_tensor(logt, rtb, logt, op=ALU.add),
                 waits=[ss.w()], inc=(sv, 1))
            P.op("vector", lambda e: e.drain())
            P.op("vector", lambda e: e.max(m8, logt))
            P.op("vector", lambda e: e.drain())
            P.op("vector", lambda e: e.max_index(i8, m8, logt))
            P.op("vector", lambda e: e.tensor_scalar(
                sc1, m8[:, 1:2], m8[:, 0:1], None, op0=ALU.subtract))
            P.op("vector", lambda e: e.drain(), inc=(sv, 1))
            P.op("scalar", lambda e: e.activation(sc2, sc1, AF.Exp),
                 waits=[sv.w()], inc=(ss, 1))
            P.op("vector", lambda e: e.tensor_scalar(
                sc1, sc2, 1.0, None, op0=ALU.add), waits=[ss.w()])
            P.op("vector", lambda e: e.drain())
            P.op("vector", lambda e: e.reciprocal(w1g, sc1))
            P.op("vector", lambda e: e.drain())
            P.op("vector", lambda e: e.tensor_scalar(
                w2g, w1g, -1.0, 1.0, op0=ALU.mult, op1=ALU.add))
            P.op("vector", lambda e: e.tensor_copy(fi2, i8[:, 0:2]))
            P.op("vector", lambda e: e.drain())
            P.op("vector", lambda e: e.tensor_scalar(
                logt, iot8, fi2[:, 0:1], None, op0=ALU.is_equal))
            P.op("vector", lambda e: e.tensor_scalar(
                m8, iot8, fi2[:, 1:2], None, op0=ALU.is_equal))
            P.op("vector", lambda e: e.drain())
            P.op("vector", lambda e: e.tensor_scalar(
                logt, logt, w1g, None, op0=ALU.mult))
            P.op("vector", lambda e: e.drain())
            P.op("vector", lambda e, tb=tb: e.scalar_tensor_tensor(
                gates[:, tb * c.ER:(tb + 1) * c.ER], m8, w2g, logt,
                op0=ALU.mult, op1=ALU.add), inc=(sv, 1))

    w1P = [wpair(0), wpair(1)]
    w3P = [wpair(2), wpair(3)]
    w2P = [wpair(4), wpair(5)]
    gs = scratch[:, 0:c.TCS]
    h_tmp = scratch[:, c.TCS:2 * c.TCS]

    def hpair(hci):
        hp = sbig[:, hci * c.T:(hci + 1) * c.T].bitcast(mybir.dt.bfloat16)
        return hp[:, 0:c.T], hp[:, c.T:2 * c.T]

    est = {
        "w1_free": [last_kq_use, last_kq_use],
        "w3_free": [last_wv_use, last_wo_use],
        "w2_free": [0, 0],
        "g_free": [0, 0],
        "u_free": [0, 0],
        "htmp_free": 0,
        "h_war": last_wo_use,
        "yb_free": {},
    }
    NTC = c.T // c.TCS

    def emit_expert(ei):
        shared = ei < c.ES
        for hg in range(c.NHG):
            h_sg = {}
            for hci in range(c.HGSZ):
                hc = hg * c.HGSZ + hci
                slot = hc % 2
                hh, hl = hpair(hci)
                w1rd = stream_pair(dw1[ei, :, hc * c.D:(hc + 1) * c.D], c.D,
                                   w1P[slot][0], w1P[slot][1],
                                   pre_wait=(st_, est["w1_free"][slot]))
                w3rd = stream_pair(dw3[ei, :, hc * c.D:(hc + 1) * c.D], c.D,
                                   w3P[slot][0], w3P[slot][1],
                                   pre_wait=(st_, est["w3_free"][slot]))
                for tc in range(NTC):
                    gb = ps[0 + (tc % 2)]
                    ub = ps[2 + (tc % 2)]
                    xrd = xfm_tb_v[min((tc + 1) * (c.TCS // 128), c.TB) - 1]
                    for dc in range(c.DC):
                        xo = dc * c.T + tc * c.TCS
                        mm3(gb[:, 0:c.TCS],
                            w1P[slot][0][:, dc * 128:(dc + 1) * 128],
                            w1P[slot][1][:, dc * 128:(dc + 1) * 128],
                            xfm_h[:, xo:xo + c.TCS], xfm_l[:, xo:xo + c.TCS],
                            start=(dc == 0), stop=(dc == c.DC - 1),
                            waits=([(sg, w1rd), (sg, xrd),
                                    (ss, est["g_free"][tc % 2])]
                                   if dc == 0 else []),
                            inc=(st_, 1) if dc == c.DC - 1 else None)
                    for dc in range(c.DC):
                        xo = dc * c.T + tc * c.TCS
                        mm3(ub[:, 0:c.TCS],
                            w3P[slot][0][:, dc * 128:(dc + 1) * 128],
                            w3P[slot][1][:, dc * 128:(dc + 1) * 128],
                            xfm_h[:, xo:xo + c.TCS], xfm_l[:, xo:xo + c.TCS],
                            start=(dc == 0), stop=(dc == c.DC - 1),
                            waits=([(sg, w3rd), (sv, est["u_free"][tc % 2])]
                                   if dc == 0 else []),
                            inc=(st_, 1) if dc == c.DC - 1 else None)
                    mm_done = st_.n
                    P.op("scalar", lambda e, b=gb: e.activation(
                        gs, b[:, 0:c.TCS], AF.Silu),
                        waits=[(st_, mm_done)], inc=(ss, 1))
                    est["g_free"][tc % 2] = ss.n
                    P.op("vector", lambda e, b=ub: e.tensor_tensor(
                        h_tmp, gs, b[:, 0:c.TCS], op=ALU.mult),
                        waits=[ss.w(), (st_, mm_done), (sg, est["htmp_free"])],
                        inc=(sv, 1))
                    est["u_free"][tc % 2] = sv.n
                    hs = tc * c.TCS
                    P.op("gpsimd", lambda e, hh=hh, hs=hs: e.tensor_copy(
                        hh[:, hs:hs + c.TCS], h_tmp),
                        waits=[sv.w(), (st_, est["h_war"])])
                    P.op("gpsimd", lambda e, hh=hh, hl=hl, hs=hs: e.tensor_tensor(
                        hl[:, hs:hs + c.TCS], h_tmp, hh[:, hs:hs + c.TCS],
                        op=ALU.subtract), inc=(sg, 1))
                    est["htmp_free"] = sg.n
                    h_sg[(hci, tc)] = sg.n
                est["w1_free"][slot] = st_.n
                est["w3_free"][slot] = st_.n
            h_g = sg.n
            w2rd = []
            for hci in range(c.HGSZ):
                hc = hg * c.HGSZ + hci
                w2rd.append(stream_pair(
                    dw2[ei, :, hc * c.D:(hc + 1) * c.D], c.D,
                    w2P[hci][0], w2P[hci][1],
                    pre_wait=(st_, est["w2_free"][hci])))
            for tc2 in range(c.T // c.TC2):
                banks = [ps[4 + i] for i in range(c.TB2 * c.ND2)]
                for hci in range(c.HGSZ):
                    hh, hl = hpair(hci)
                    for tb2 in range(c.TB2):
                        for d2 in range(c.ND2):
                            b = banks[tb2 * c.ND2 + d2]
                            bi = tb2 * c.ND2 + d2
                            tcol = tc2 * c.TC2 + tb2 * 128
                            waits = []
                            if hci == 0:
                                waits = [(sg, w2rd[-1]), (sg, h_g)]
                                if bi in est["yb_free"]:
                                    waits.append((sv, est["yb_free"][bi]))
                            mm3(b[:, 0:c.N2],
                                hh[:, tcol:tcol + 128], hl[:, tcol:tcol + 128],
                                w2P[hci][0][:, d2 * c.N2:(d2 + 1) * c.N2],
                                w2P[hci][1][:, d2 * c.N2:(d2 + 1) * c.N2],
                                start=(hci == 0), stop=(hci == c.HGSZ - 1),
                                waits=waits,
                                inc=(st_, 1) if hci == c.HGSZ - 1 else None)
                y_done = st_.n
                for tb2 in range(c.TB2):
                    gtb = (tc2 * c.TC2) // 128 + tb2
                    for d2 in range(c.ND2):
                        b = banks[tb2 * c.ND2 + d2]
                        bi = tb2 * c.ND2 + d2
                        col = gtb * c.D + d2 * c.N2
                        if shared:
                            P.op("vector", lambda e, b=b, col=col: e.tensor_tensor(
                                big2[:, col:col + c.N2], b[:, 0:c.N2],
                                big2[:, col:col + c.N2], op=ALU.add),
                                waits=[(st_, y_done)], inc=(sv, 1))
                        else:
                            gate = gates[:, gtb * c.ER + (ei - c.ES):
                                         gtb * c.ER + (ei - c.ES) + 1]
                            P.op("vector", lambda e, b=b, col=col, gate=gate:
                                 e.scalar_tensor_tensor(
                                     big2[:, col:col + c.N2], b[:, 0:c.N2], gate,
                                     big2[:, col:col + c.N2],
                                     op0=ALU.mult, op1=ALU.add),
                                 waits=[(st_, y_done)], inc=(sv, 1))
                        est["yb_free"][bi] = sv.n
            for hci in range(c.HGSZ):
                est["w2_free"][hci] = st_.n
            est["h_war"] = st_.n

    for ei in range(c.ES):
        emit_expert(ei)
    emit_router()
    for ei in range(c.ES, c.NE):
        emit_expert(ei)

    for tb in range(c.TB):
        ln_stats(big2[:, tb * c.D:(tb + 1) * c.D], c.D, waits=[(sv, sv.n)])
        ln_apply(rm, big2[:, tb * c.D:(tb + 1) * c.D], waits=[])
        out_v = sv.n
        P.op("sync", lambda e, tb=tb: e.dma_start(
            out=dy[tb * 128:(tb + 1) * 128, :], in_=rm),
            waits=[(sv, out_v)], inc=(so, 16))
        P.barrier("vector", waits=[(so, so.n)])

    P.barrier("sync", waits=[(so, so.n)])

    if debug_dump:
        dxf = nc.dram_tensor("dbg_xfm", [128, c.DC * c.T], F32, kind="ExternalOutput").ap()
        dbg = nc.dram_tensor("dbg_big2", [128, c.TB * c.D], F32, kind="ExternalOutput").ap()
        dsb = nc.dram_tensor("dbg_sbig", [128, sb_len], F32, kind="ExternalOutput").ap()
        dsc = nc.dram_tensor("dbg_scr", [128, 2 * c.D], F32, kind="ExternalOutput").ap()
        P.op("sync", lambda e: e.dma_start(out=dxf, in_=xfm),
             waits=[(sv, sv.n), (ss, ss.n), (st_, st_.n), (sg, sg.n), (sw, sw.n)],
             inc=(so, 16))
        P.op("sync", lambda e: e.dma_start(out=dbg, in_=big2), inc=(so, 16))
        P.op("sync", lambda e: e.dma_start(out=dsb, in_=sbig), inc=(so, 16))
        P.op("sync", lambda e: e.dma_start(out=dsc, in_=scratch), inc=(so, 16))
        P.op("sync", lambda e: e.dma_start(out=dgt, in_=gates), inc=(so, 16))
        P.barrier("sync", waits=[(so, so.n)])

    _finish(nc, P, sems)
    return nc



def stage_inputs(cfg, ins):
    c = cfg
    H, DH, D, HID = c.H, c.DH, c.D, c.HID
    WQ, WK, WV = ins["L0_attn_W_Q"], ins["L0_attn_W_K"], ins["L0_attn_W_V"]
    WO = ins["L0_attn_W_O"]

    def pair_lhsT(W):
        cols = []
        for p in range(c.PAIRS):
            m = np.concatenate([W[2 * p], W[2 * p + 1]], axis=1)
            m = m.reshape(c.DC, 128, 128).transpose(1, 0, 2).reshape(128, c.D)
            cols.append(m)
        return np.ascontiguousarray(np.concatenate(cols, axis=1), dtype=np.float32)

    wk = pair_lhsT(WK)
    wq = pair_lhsT(WQ)
    wv = pair_lhsT(WV)
    WOf = WO.reshape(H * DH, D)
    wo = np.ascontiguousarray(
        WOf.reshape(c.PAIRS, 128, D).transpose(1, 0, 2).reshape(128, c.PAIRS * D),
        dtype=np.float32)
    bk = np.ascontiguousarray(np.stack(
        [np.concatenate([ins["L0_attn_b_K"][2 * p], ins["L0_attn_b_K"][2 * p + 1]])
         for p in range(c.PAIRS)], axis=1), dtype=np.float32)
    bq = np.ascontiguousarray(np.stack(
        [np.concatenate([ins["L0_attn_b_Q"][2 * p], ins["L0_attn_b_Q"][2 * p + 1]])
         for p in range(c.PAIRS)], axis=1), dtype=np.float32)
    bv = np.ascontiguousarray(np.stack(
        [np.concatenate([ins["L0_attn_b_V"][2 * p], ins["L0_attn_b_V"][2 * p + 1]])
         for p in range(c.PAIRS)], axis=1), dtype=np.float32)
    bo = np.ascontiguousarray(ins["L0_attn_b_O"][None, :], dtype=np.float32)
    l1w = np.ascontiguousarray(
        ins["L0_attn_ln1_w"].reshape(c.DC, 128).T, dtype=np.float32)
    l1b = np.ascontiguousarray(
        ins["L0_attn_ln1_b"].reshape(c.DC, 128).T, dtype=np.float32)
    l2w = np.ascontiguousarray(
        np.broadcast_to(ins["L0_attn_ln2_w"][None, :], (128, D)), dtype=np.float32)
    l2b = np.ascontiguousarray(
        np.broadcast_to(ins["L0_attn_ln2_b"][None, :], (128, D)), dtype=np.float32)
    rw = ins["L0_ffn_router_w"]
    rtw = np.ascontiguousarray(
        rw.T.reshape(c.DC, 128, c.ER).transpose(1, 0, 2).reshape(128, c.DC * c.ER),
        dtype=np.float32)
    rtb = np.ascontiguousarray(
        np.broadcast_to(ins["L0_ffn_expert_bias"][None, :], (128, c.ER)),
        dtype=np.float32)

    def stage_w13(w):
        m = w.T
        m = m.reshape(c.DC, 128, c.HC, 128).transpose(1, 2, 0, 3)
        return m.reshape(128, c.HC * c.DC * 128)

    def stage_w2(w):
        m = w.T
        m = m.reshape(c.HC, 128, D).transpose(1, 0, 2)
        return m.reshape(128, c.HC * D)

    e_w1 = [ins["L0_ffn_sw1"][i] for i in range(c.ES)] + \
           [ins["L0_ffn_w1"][i] for i in range(c.ER)]
    e_w3 = [ins["L0_ffn_sw3"][i] for i in range(c.ES)] + \
           [ins["L0_ffn_w3"][i] for i in range(c.ER)]
    e_w2 = [ins["L0_ffn_sw2"][i] for i in range(c.ES)] + \
           [ins["L0_ffn_w2"][i] for i in range(c.ER)]
    w1s = np.ascontiguousarray(np.stack([stage_w13(np.asarray(w)) for w in e_w1]),
                               dtype=np.float32)
    w3s = np.ascontiguousarray(np.stack([stage_w13(np.asarray(w)) for w in e_w3]),
                               dtype=np.float32)
    w2s = np.ascontiguousarray(np.stack([stage_w2(np.asarray(w)) for w in e_w2]),
                               dtype=np.float32)
    return {
        "wk": wk, "wq": wq, "wv": wv, "wo": wo, "bk": bk, "bq": bq, "bv": bv,
        "bo": bo, "ln1w": l1w, "ln1b": l1b, "ln2w": l2w, "ln2b": l2b,
        "rtw": rtw, "rtb": rtb, "w1s": w1s, "w3s": w3s, "w2s": w2s,
    }


def kernel(**inputs):
    cfg = FULL
    x = np.ascontiguousarray(np.asarray(inputs["L0_attn_resid_pre"],
                                        dtype=np.float32))
    B = x.shape[0]
    shared = stage_inputs(cfg, {k: np.asarray(v) for k, v in inputs.items()})
    nc = build_nc(cfg)
    in_maps = []
    for b in range(B):
        m = dict(shared)
        m["x0"] = np.ascontiguousarray(x[b])
        in_maps.append(m)
    res = run_bass_kernel_spmd(nc, in_maps, core_ids=list(range(B)))
    return np.stack([res.results[b]["y"] for b in range(B)], axis=0)


# version 17
# speedup vs baseline: 1.0988x; 1.0988x over previous
import math
import numpy as np
import concourse.bass as bass
import concourse.mybir as mybir
from concourse.bass_utils import run_bass_kernel_spmd

F32 = mybir.dt.float32
U32 = mybir.dt.uint32
AF = mybir.ActivationFunctionType
ALU = mybir.AluOpType
AX = mybir.AxisListType

LN_EPS = 1e-5
CSHIFT = 8.0
EXP_NEG_C = float(np.float32(math.exp(-CSHIFT)))


class Cfg:
    def __init__(self, T=2048, D=1024, H=16, HID=2048, ER=8, ES=2,
                 TCS=512, TC2=256, HGSZ=2):
        self.T, self.D, self.H, self.HID = T, D, H, HID
        self.ER, self.ES = ER, ES
        self.DH = 64
        self.TB = T // 128
        self.DC = D // 128
        self.PAIRS = H // 2
        self.HC = HID // 128
        self.TCS = min(TCS, T)
        self.NQC = T // self.TCS
        self.TC2 = min(TC2, T)
        self.TB2 = self.TC2 // 128
        self.N2 = min(512, D)
        self.ND2 = D // self.N2
        self.HGSZ = HGSZ
        self.NHG = self.HC // HGSZ
        self.NE = ES + ER
        assert T % 128 == 0 and D % 128 == 0 and HID % 128 == 0
        assert self.TB2 * self.ND2 <= 4
        assert self.ER == 8, "router uses max8 (needs exactly >=8 logits)"


FULL = Cfg()


class Sem:
    def __init__(self, h):
        self.h = h
        self.n = 0

    def inc(self, k=1):
        self.n += k
        return self.n

    def w(self):
        return (self, self.n)


class Prog:

    def __init__(self):
        self.ops = []

    def op(self, eng, fn, waits=(), inc=None):
        waits = [(s, v) for (s, v) in waits if v > 0]
        if inc is not None:
            sem, k = inc
            sem.inc(k)
        self.ops.append((eng, waits, fn, inc))

    def barrier(self, eng, waits):
        self.op(eng, None, waits=waits)

    def replay(self, block, engines):
        streams = {name: [] for name in engines}
        for (eng, waits, fn, inc) in self.ops:
            streams[eng].append((waits, fn, inc))

        def make_body(name):
            def body(e):
                last_wait = {}
                for (waits, fn, inc) in streams[name]:
                    for (sem, val) in waits:
                        if last_wait.get(id(sem), -1) >= val:
                            continue
                        e.wait_ge(sem.h, val)
                        last_wait[id(sem)] = val
                    if fn is None:
                        assert inc is None
                        continue
                    inst = fn(e)
                    if inc is not None:
                        sem, k = inc
                        inst.then_inc(sem.h, k)
            return body
        for name in engines:
            getattr(block, name)(make_body(name))



def _finish(nc, P, sems):
    nums = sorted(s.h.num for s in sems.values())
    assert nums == list(range(nums[0], nums[0] + len(nums))), nums
    rng = range(nums[0], nums[-1] + 1)
    with nc.Block() as block:
        nc.all_engine_barrier()
        nc.gpsimd.dma_reset(rng)
        nc.gpsimd.sem_clear(rng)
        nc.all_engine_barrier()
        P.replay(block, ["sync", "gpsimd", "scalar", "vector", "tensor"])


def build_nc(cfg: Cfg, phases="ABCDEF", debug_dump=False):
    c = cfg
    nc = bass.Bass("TRN2", target_bir_lowering=False,
                   detect_race_conditions=False, dynamic_dma_scratch_size=4096)

    dx0 = nc.dram_tensor("x0", [c.T, c.D], F32, kind="ExternalInput").ap()
    dwk = nc.dram_tensor("wk", [128, c.PAIRS * c.D], F32, kind="ExternalInput").ap()
    dwq = nc.dram_tensor("wq", [128, c.PAIRS * c.D], F32, kind="ExternalInput").ap()
    dwv = nc.dram_tensor("wv", [128, c.PAIRS * c.D], F32, kind="ExternalInput").ap()
    dwo = nc.dram_tensor("wo", [128, c.PAIRS * c.D], F32, kind="ExternalInput").ap()
    dbk = nc.dram_tensor("bk", [128, c.PAIRS], F32, kind="ExternalInput").ap()
    dbq = nc.dram_tensor("bq", [128, c.PAIRS], F32, kind="ExternalInput").ap()
    dbv = nc.dram_tensor("bv", [128, c.PAIRS], F32, kind="ExternalInput").ap()
    dbo = nc.dram_tensor("bo", [1, c.D], F32, kind="ExternalInput").ap()
    dl1w = nc.dram_tensor("ln1w", [128, c.DC], F32, kind="ExternalInput").ap()
    dl1b = nc.dram_tensor("ln1b", [128, c.DC], F32, kind="ExternalInput").ap()
    dl2w = nc.dram_tensor("ln2w", [128, c.D], F32, kind="ExternalInput").ap()
    dl2b = nc.dram_tensor("ln2b", [128, c.D], F32, kind="ExternalInput").ap()
    drtw = nc.dram_tensor("rtw", [128, c.DC * c.ER], F32, kind="ExternalInput").ap()
    drtb = nc.dram_tensor("rtb", [128, c.ER], F32, kind="ExternalInput").ap()
    dw1 = nc.dram_tensor("w1s", [c.NE, 128, c.HC * c.D], F32, kind="ExternalInput").ap()
    dw3 = nc.dram_tensor("w3s", [c.NE, 128, c.HC * c.D], F32, kind="ExternalInput").ap()
    dw2 = nc.dram_tensor("w2s", [c.NE, 128, c.HC * c.D], F32, kind="ExternalInput").ap()
    dy = nc.dram_tensor("y", [c.T, c.D], F32, kind="ExternalOutput").ap()

    A = lambda name, shape, dt=F32: nc.alloc_sbuf_tensor(name, shape, dt).ap()
    BF16 = mybir.dt.bfloat16
    xfmP = nc.alloc_sbuf_tensor("xfmP", [128, 2 * c.DC * c.T], BF16).ap()
    xfm_h = xfmP[:, 0:c.DC * c.T]
    xfm_l = xfmP[:, c.DC * c.T:2 * c.DC * c.T]
    big2 = A("big2", [128, c.TB * c.D])
    sb_len = max(3 * c.T + c.TB * 130, c.HGSZ * c.T)
    sbig = A("sbig", [128, sb_len])
    kpair = sbig[:, 0:c.T]
    qpair = sbig[:, c.T:2 * c.T]
    zpair = sbig[:, 2 * c.T:3 * c.T]
    vt_off = 3 * c.T
    WBW = 12 * c.D + c.D // 2
    wbuf = nc.alloc_sbuf_tensor("wbufB", [128, WBW], BF16).ap()

    def wpair(i):
        return (wbuf[:, 2 * i * c.D:(2 * i + 1) * c.D],
                wbuf[:, (2 * i + 1) * c.D:(2 * i + 2) * c.D])
    wo_f32 = wbuf[:, 6 * c.D:8 * c.D].bitcast(F32)
    LQ = c.D // 8
    land = [wbuf[:, 12 * c.D + 2 * i * LQ:12 * c.D + 2 * (i + 1) * LQ].bitcast(F32)
            for i in range(2)]
    NJ = c.TCS // 128
    WCW = NJ * 128 + c.TCS
    ovlrp = A("ovlrp", [128, WCW + max(c.D, 4 * c.TCS)])
    cptW = ovlrp[:, 0:WCW]
    cpt = cptW[:, 0:c.TCS]
    ptile = ovlrp[:, WCW:WCW + 4 * c.TCS]
    resid = ovlrp[:, WCW:WCW + c.D]
    scratch = A("scratch", [128, max(2 * c.D, c.D + c.TCS + 130)])
    l2w = A("l2wr", [128, c.D])
    l2b = A("l2br", [128, c.D])
    ident = A("ident", [128, 128])
    onesr = A("onesr", [1, 128])
    bkt = A("bkt", [128, c.PAIRS])
    bqt = A("bqt", [128, c.PAIRS])
    bvc = A("bvc", [128, c.PAIRS])
    bot = A("bot", [1, c.D])
    l1w = A("l1w", [128, c.DC])
    l1b = A("l1b", [128, c.DC])
    rtwP = nc.alloc_sbuf_tensor("rtwP", [128, 2 * c.DC * c.ER],
                                mybir.dt.bfloat16).ap()
    rtb = A("rtbr", [128, c.ER])
    iot8 = A("iot8", [128, c.ER])
    gates = A("gates", [128, c.TB * c.ER])
    m8 = A("m8", [128, 8])
    i8 = A("i8", [128, 8], U32)
    fi2 = A("fi2", [128, 2])
    sc1 = A("sc1", [128, 1])
    sc2 = A("sc2", [128, 1])
    w1g = A("w1g", [128, 1])
    w2g = A("w2g", [128, 1])
    bn12 = A("bn12", [128, 12])
    mv2 = A("mv2", [128, 2])
    rstd = A("rstd", [128, 1])
    negm = A("negm", [128, 1])
    logt = A("logt", [128, c.ER])
    cneg8 = A("cneg8", [128, 1])
    dstrip = A("dstrip", [128, c.TB * 16]) if debug_dump else None
    rden = scratch[0:1, 0:2 * c.TCS]
    sbc = scratch[:, c.D:c.D + c.TCS]
    vsuf = scratch[:, c.D + c.TCS:c.D + c.TCS + 130]

    ps = [nc.alloc_psum_tensor(f"ps{i}", [128, 512], F32).ap() for i in range(8)]

    sems = {}
    for name in ["w", "x", "o", "t", "s", "v", "g"]:
        sems[name] = Sem(nc.semaphore(f"sem_{name}").__enter__())
    sw, sx, so, st_, ss, sv, sg = (sems[k] for k in ["w", "x", "o", "t", "s", "v", "g"])

    P = Prog()

    land_free = [0, 0]
    land_rr = [0]

    def stream_pair(src_ap, W, dst_h, dst_l, pre_wait=None):
        nq = W // LQ
        first = True
        for q in range(nq):
            sl = land_rr[0] % 2
            land_rr[0] += 1
            P.op("sync", lambda e, q=q, sl=sl, s=src_ap: e.dma_start(
                out=land[sl], in_=s[:, q * LQ:(q + 1) * LQ]),
                waits=[(sg, land_free[sl])], inc=(sw, 16))
            wv_ = sw.n
            waits = [(sw, wv_)]
            if first and pre_wait is not None:
                waits.append(pre_wait)
                first = False
            P.op("scalar", lambda e, q=q, sl=sl, d=dst_h: e.activation(
                d[:, q * LQ:(q + 1) * LQ], land[sl], AF.Copy),
                waits=waits, inc=(ss, 1))
            hw_ = ss.n
            P.op("gpsimd", lambda e, q=q, sl=sl, dh=dst_h, dl=dst_l: e.tensor_tensor(
                dl[:, q * LQ:(q + 1) * LQ], land[sl],
                dh[:, q * LQ:(q + 1) * LQ], op=ALU.subtract),
                waits=[(ss, hw_)], inc=(sg, 1))
            land_free[sl] = sg.n
        return sg.n

    def mm3(out_ap, lh, ll, rh, rl, start, stop, waits=(), inc=None):
        P.op("tensor", lambda e, o=out_ap, a=lh, b=rh, s=start: e.matmul(
            o, a, b, start=s, stop=False), waits=waits)
        P.op("tensor", lambda e, o=out_ap, a=ll, b=rh: e.matmul(
            o, a, b, start=False, stop=False))
        P.op("tensor", lambda e, o=out_ap, a=lh, b=rl, s=stop: e.matmul(
            o, a, b, start=False, stop=s), inc=inc)

    def vtok(tb, half):
        off = vt_off + tb * 130 + half * 65
        return sbig[:, off:off + 65]

    for (dst, src) in [(bkt, dbk), (bqt, dbq), (bvc, dbv), (bot, dbo),
                       (l1w, dl1w), (l1b, dl1b), (l2w, dl2w), (l2b, dl2b),
                       (rtb, drtb)]:
        P.op("sync", lambda e, d=dst, s=src: e.dma_start(out=d, in_=s), inc=(sw, 16))
    n_small = 9

    P.op("gpsimd", lambda e: e.memset(ident, 1.0))
    P.op("gpsimd", lambda e: e.affine_select(
        ident, ident, pattern=[[1, 128]], compare_op=ALU.is_equal,
        fill=0.0, base=0, channel_multiplier=-1), inc=(sg, 1))
    P.op("gpsimd", lambda e: e.memset(onesr, 1.0))
    P.op("gpsimd", lambda e: e.memset(cptW, -CSHIFT), inc=(sg, 1))
    P.op("scalar", lambda e: e.activation(cptW, cptW, AF.Exp),
         waits=[sg.w()], inc=(ss, 1))
    P.op("gpsimd", lambda e: e.affine_select(
        cptW, cptW, pattern=[[-1, WCW]], compare_op=ALU.is_ge,
        fill=0.0, base=NJ * 128 - 1, channel_multiplier=1),
        waits=[ss.w()])
    P.op("gpsimd", lambda e: e.iota(iot8, pattern=[[1, c.ER]], channel_multiplier=0, allow_small_or_imprecise_dtypes=True), inc=(sg, 1))
    P.op("gpsimd", lambda e: e.memset(cneg8, -CSHIFT))
    NRW = c.DC * c.ER
    P.op("sync", lambda e: e.dma_start(out=land[0][:, 0:NRW], in_=drtw),
         inc=(sw, 16))
    rtw_w = sw.n
    P.op("gpsimd", lambda e: e.tensor_copy(rtwP[:, 0:NRW], land[0][:, 0:NRW]),
         waits=[(sw, rtw_w)])
    P.op("gpsimd", lambda e: e.tensor_tensor(
        rtwP[:, NRW:2 * NRW], land[0][:, 0:NRW], rtwP[:, 0:NRW],
        op=ALU.subtract), inc=(sg, 1))
    land_free[0] = sg.n
    P.op("gpsimd", lambda e: e.memset(
        sbig[:, vt_off:vt_off + c.TB * 130].rearrange(
            "p (t o) -> p t o", o=130)[:, :, 64:65], 1.0))
    P.op("gpsimd", lambda e: e.memset(
        sbig[:, vt_off:vt_off + c.TB * 130].rearrange(
            "p (t o) -> p t o", o=130)[:, :, 129:130], 1.0), inc=(sg, 1))
    setup_g = sg.n

    def ln_stats(src_ap, width, waits):
        nchunk = (width + 511) // 512
        assert nchunk in (1, 2)
        for i in range(nchunk):
            w0 = i * 512
            w1 = min(width, w0 + 512)
            P.op("vector", lambda e, a=src_ap, w0=w0, w1=w1, i=i: e.bn_stats(
                bn12[:, 6 * i:6 * i + 6], a[:, w0:w1]), waits=waits)
        P.op("vector", lambda e: e.drain())
        P.op("vector", lambda e, n=nchunk: e.bn_aggr(mv2, bn12[:, 0:6 * n]))
        P.op("vector", lambda e: e.drain())
        P.op("vector", lambda e: e.tensor_scalar(
            sc1, mv2[:, 1:2], LN_EPS, None, op0=ALU.add), inc=(sv, 1))
        P.op("scalar", lambda e: e.activation(sc2, sc1, AF.Sqrt),
             waits=[sv.w()], inc=(ss, 1))
        P.op("vector", lambda e: e.reciprocal(rstd, sc2), waits=[ss.w()])
        P.op("vector", lambda e: e.drain())
        P.op("vector", lambda e: e.tensor_scalar(
            negm, mv2[:, 0:1], rstd, -1.0, op0=ALU.mult, op1=ALU.mult))
        P.op("vector", lambda e: e.drain(), inc=(sv, 1))

    def ln_apply(dst_ap, src_ap, waits):
        P.op("vector", lambda e: e.tensor_scalar(
            dst_ap, src_ap, mv2[:, 0:1], rstd, op0=ALU.subtract, op1=ALU.mult),
            waits=waits, inc=(sv, 1))

    xn_a = scratch[:, 0:c.D]
    tmp_free = [0, 0]
    for tb in range(c.TB):
        P.op("scalar", lambda e, tb=tb: e.dma_start(
            out=resid, in_=dx0[tb * 128:(tb + 1) * 128, :]),
            waits=[sv.w()], inc=(sx, 16))
        ln_stats(resid, c.D, waits=[(sx, sx.n)])
        ln_apply(xn_a, resid, waits=[])
        for dc in range(c.DC):
            bank = ps[6 + (dc % 2)]
            P.op("tensor", lambda e, b=bank, dc=dc: e.transpose(
                b[:, 0:128], xn_a[:, dc * 128:(dc + 1) * 128], ident),
                waits=[(sv, sv.n), (sg, setup_g)], inc=(st_, 1))
            col = dc * c.T + tb * 128
            tmp = scratch[:, c.D + (dc % 2) * 128:c.D + (dc % 2) * 128 + 128]
            P.op("vector", lambda e, b=bank, t=tmp, dc=dc: e.tensor_scalar(
                t, b[:, 0:128],
                l1w[:, dc:dc + 1], l1b[:, dc:dc + 1], op0=ALU.mult, op1=ALU.add),
                waits=[st_.w(), (sw, n_small * 16), (sg, tmp_free[dc % 2])],
                inc=(sv, 1))
            P.op("gpsimd", lambda e, t=tmp, col=col: e.tensor_copy(
                xfm_h[:, col:col + 128], t), waits=[sv.w()])
            P.op("gpsimd", lambda e, t=tmp, col=col: e.tensor_tensor(
                xfm_l[:, col:col + 128], t, xfm_h[:, col:col + 128],
                op=ALU.subtract), inc=(sg, 1))
            tmp_free[dc % 2] = sg.n

    if "B" not in phases:
        if debug_dump:
            dxf = nc.dram_tensor("dbg_xfm", [128, c.DC * c.T], F32, kind="ExternalOutput").ap()
            dbg = nc.dram_tensor("dbg_big2", [128, c.TB * c.D], F32, kind="ExternalOutput").ap()
            dsb = nc.dram_tensor("dbg_sbig", [128, sb_len], F32, kind="ExternalOutput").ap()
            dsc = nc.dram_tensor("dbg_scr", [128, 2 * c.D], F32, kind="ExternalOutput").ap()
            P.op("sync", lambda e: e.dma_start(out=dxf, in_=xfm),
                 waits=[(sv, sv.n), (ss, ss.n), (st_, st_.n), (sg, sg.n), (sw, sw.n)],
                 inc=(so, 16))
            P.op("sync", lambda e: e.dma_start(out=dbg, in_=big2), inc=(so, 16))
            P.op("sync", lambda e: e.dma_start(out=dsb, in_=sbig), inc=(so, 16))
            P.op("sync", lambda e: e.dma_start(out=dsc, in_=scratch), inc=(so, 16))
            dgt = nc.dram_tensor("dbg_gates", [128, c.TB * c.ER], F32, kind="ExternalOutput").ap()
            P.op("sync", lambda e: e.dma_start(out=dgt, in_=gates), inc=(so, 16))
            dst2 = nc.dram_tensor("dbg_strip", [128, c.TB * 16], F32, kind="ExternalOutput").ap()
            P.op("sync", lambda e: e.dma_start(out=dst2, in_=dstrip), inc=(so, 16))
            P.barrier("sync", waits=[(so, so.n)])

        _finish(nc, P, sems)
        return nc

    wkP, wqP, wvP = wpair(0), wpair(1), wpair(2)
    last_kq_use = 0
    last_wv_use = 0
    last_wo_use = 0
    NKB = c.T // 128

    for p in range(c.PAIRS):
        kq_rdy = stream_pair(dwk[:, p * c.D:(p + 1) * c.D], c.D, wkP[0], wkP[1],
                             pre_wait=(st_, last_kq_use))
        kq_rdy = stream_pair(dwq[:, p * c.D:(p + 1) * c.D], c.D, wqP[0], wqP[1],
                             pre_wait=(st_, last_kq_use))
        wv_rdy = stream_pair(dwv[:, p * c.D:(p + 1) * c.D], c.D, wvP[0], wvP[1],
                             pre_wait=(st_, last_wv_use))
        P.op("sync", lambda e, p=p: e.dma_start(
            out=wo_f32, in_=dwo[:, p * c.D:(p + 1) * c.D]),
            waits=[(st_, last_wo_use)], inc=(sw, 16))
        w_ready = sw.n

        for (wsrc, dst, bias) in [(wkP, kpair, bkt), (wqP, qpair, bqt)]:
            for tc in range(c.NQC):
                bank = ps[6 + (tc % 2)]
                for dc in range(c.DC):
                    xo = dc * c.T + tc * c.TCS
                    mm3(bank[:, 0:c.TCS],
                        wsrc[0][:, dc * 128:(dc + 1) * 128],
                        wsrc[1][:, dc * 128:(dc + 1) * 128],
                        xfm_h[:, xo:xo + c.TCS], xfm_l[:, xo:xo + c.TCS],
                        start=(dc == 0), stop=(dc == c.DC - 1),
                        waits=([(sg, kq_rdy), (sv, sv.n), (ss, ss.n)]
                               if dc == 0 else []),
                        inc=(st_, 1) if dc == c.DC - 1 else None)
                P.op("scalar", lambda e, b=bank, d=dst, tc=tc, bias=bias, p=p:
                     e.activation(d[:, tc * c.TCS:(tc + 1) * c.TCS],
                                  b[:, 0:c.TCS], AF.Identity,
                                  bias=bias[:, p:p + 1], scale=1.0),
                     waits=[st_.w()], inc=(ss, 1))
        last_kq_use = st_.n

        for tb in range(c.TB):
            bank = ps[6 + (tb % 2)]
            for dc in range(c.DC):
                xo = dc * c.T + tb * 128
                mm3(bank[:, 0:128],
                    xfm_h[:, xo:xo + 128], xfm_l[:, xo:xo + 128],
                    wvP[0][:, dc * 128:(dc + 1) * 128],
                    wvP[1][:, dc * 128:(dc + 1) * 128],
                    start=(dc == 0), stop=(dc == c.DC - 1),
                    waits=[(ss, ss.n), (sg, wv_rdy)] if dc == 0 else [],
                    inc=(st_, 1) if dc == c.DC - 1 else None)
            off = vt_off + tb * 130
            P.op("scalar", lambda e, b=bank, off=off: e.copy(
                sbig[:, off:off + 64], b[:, 0:64]),
                waits=[st_.w(), (sg, setup_g)], inc=(ss, 1))
            P.op("scalar", lambda e, b=bank, off=off: e.copy(
                sbig[:, off + 65:off + 129], b[:, 64:128]), inc=(ss, 1))
        last_wv_use = st_.n

        zevac_v = sv.n
        qk_ready = ss.n
        bank_free = {}
        bc_free = {0: 0, 1: 0}
        suf_start = NKB
        vsuf_v = 0
        for qc in range(c.NQC - 1, -1, -1):
            nkb_act = min(NKB, (qc + 1) * c.TCS // 128)
            if nkb_act < suf_start:
                for kb in range(suf_start - 1, nkb_act - 1, -1):
                    for h in range(2):
                        vs = vsuf[:, h * 65:(h + 1) * 65]
                        if suf_start == NKB and kb == suf_start - 1:
                            pass
                        if kb == NKB - 1:
                            P.op("vector", lambda e, vs=vs, kb=kb, h=h: e.tensor_copy(
                                vs, vtok(kb, h)), waits=[(ss, qk_ready)])
                        else:
                            P.op("vector", lambda e, vs=vs, kb=kb, h=h: e.tensor_tensor(
                                vs, vtok(kb, h), vs, op=ALU.add),
                                waits=[(ss, qk_ready)])
                suf_start = nkb_act
                P.op("vector", lambda e: e.drain(), inc=(sv, 1))
                vsuf_v = sv.n
            zb = [ps[4], ps[5]]
            exp_done = {}

            def emit_S(kb, h, qc=qc):
                sb = ps[2 * h + (kb % 2)]
                lo, hi = 64 * h, 64 * h + 64
                waits = [(ss, qk_ready)]
                if (h, kb % 2) in bank_free:
                    waits.append((ss, bank_free[(h, kb % 2)]))
                if kb % 2 == 0 and bc_free[h]:
                    waits.append((ss, bc_free[h]))
                P.op("tensor",
                     lambda e, b=sb, kb=kb, qc=qc, lo=lo, hi=hi: e.matmul(
                         b[:, 0:c.TCS],
                         kpair[lo:hi, kb * 128:(kb + 1) * 128],
                         qpair[lo:hi, qc * c.TCS:(qc + 1) * c.TCS],
                         start=True, stop=True),
                     waits=waits, inc=(st_, 1))
                s_done = st_.n
                pt = ptile[:, (2 * h + kb % 2) * c.TCS:
                           (2 * h + kb % 2 + 1) * c.TCS]
                base = qc * c.TCS - kb * 128
                if base < 127:
                    j = kb - qc * NJ
                    assert 0 <= j < NJ and base == -128 * j
                    if j > 0:
                        P.op("gpsimd", lambda e, pt=pt, j=j: e.tensor_copy(
                            pt[:, 0:128 * j], cpt[:, 0:128 * j]),
                            waits=[(st_, s_done)])
                    P.op("scalar", lambda e, b=sb, pt=pt, j=j: e.activation(
                        pt[:, 128 * j:c.TCS], b[:, 128 * j:c.TCS],
                        AF.Exp, bias=cneg8, scale=0.125),
                        waits=[(st_, s_done)], inc=(ss, 1))
                    bank_free[(h, kb % 2)] = ss.n
                    P.op("gpsimd", lambda e, pt=pt, j=j: e.affine_select(
                        pt[:, 128 * j:128 * j + 128],
                        pt[:, 128 * j:128 * j + 128],
                        pattern=[[1, 128]], compare_op=ALU.is_ge,
                        fill=0.0, base=0, channel_multiplier=-1),
                        waits=[ss.w()])
                    cwn = cptW[:, NJ * 128:NJ * 128 + 128]
                    P.op("gpsimd", lambda e, pt=pt, j=j, cwn=cwn: e.tensor_tensor(
                        pt[:, 128 * j:128 * j + 128],
                        pt[:, 128 * j:128 * j + 128], cwn, op=ALU.add),
                        inc=(sg, 1))
                    exp_done[(kb, h)] = ("g", sg.n, pt)
                else:
                    P.op("scalar", lambda e, b=sb, pt=pt: e.activation(
                        pt, b[:, 0:c.TCS], AF.Exp, bias=cneg8, scale=0.125),
                        waits=[(st_, s_done)], inc=(ss, 1))
                    bank_free[(h, kb % 2)] = ss.n
                    exp_done[(kb, h)] = ("s", ss.n, pt)

            last_active_stops = (nkb_act == NKB)

            def emit_z(kb, h):
                who, val, pt = exp_done[(kb, h)]
                waits = [(sg, val) if who == "g" else (ss, val)]
                if kb == 0:
                    waits.append((sv, zevac_v))
                stop = last_active_stops and kb == NKB - 1
                P.op("tensor", lambda e, zb=zb[h], kb=kb, h=h, pt=pt, stop=stop:
                     e.matmul(zb[0:65, 0:c.TCS], vtok(kb, h), pt,
                              start=(kb == 0), stop=stop),
                     waits=waits, inc=(st_, 1) if stop else None)

            for kb in range(nkb_act):
                emit_S(kb, 0)
                emit_S(kb, 1)
                if kb > 0:
                    emit_z(kb - 1, 0)
                    emit_z(kb - 1, 1)
            emit_z(nkb_act - 1, 0)
            emit_z(nkb_act - 1, 1)
            if not last_active_stops:
                for h in range(2):
                    P.op("tensor", lambda e, zb=zb[h], h=h: e.matmul(
                        zb[0:65, 0:c.TCS], vsuf[:, h * 65:(h + 1) * 65], cpt,
                        start=False, stop=True),
                        waits=[(sv, vsuf_v)], inc=(st_, 1))
            z_done = st_.n
            for h in range(2):
                P.op("vector", lambda e, h=h, zb=zb[h]: e.reciprocal(
                    rden[0:1, h * c.TCS:(h + 1) * c.TCS],
                    zb[64:65, 0:c.TCS]), waits=[(st_, z_done)], inc=(sv, 1))
            recip_v = sv.n
            for h in range(2):
                P.op("tensor", lambda e, h=h: e.matmul(
                    ps[2 * h][0:64, 0:c.TCS], onesr[0:1, 0:64],
                    rden[0:1, h * c.TCS:(h + 1) * c.TCS],
                    start=True, stop=True), waits=[(sv, recip_v)], inc=(st_, 1))
            bc_done = st_.n
            for h in range(2):
                P.op("scalar", lambda e, h=h: e.copy(
                    sbc[64 * h:64 * h + 64, 0:c.TCS],
                    ps[2 * h][0:64, 0:c.TCS]), waits=[(st_, bc_done)], inc=(ss, 1))
                bc_free[h] = ss.n
            sbc_done = ss.n
            for h in range(2):
                P.op("vector", lambda e, h=h, zb=zb[h], qc=qc: e.tensor_tensor(
                    zpair[64 * h:64 * h + 64, qc * c.TCS:(qc + 1) * c.TCS],
                    zb[0:64, 0:c.TCS], sbc[64 * h:64 * h + 64, 0:c.TCS],
                    op=ALU.mult), waits=[(ss, sbc_done)])
            P.op("vector", lambda e: e.drain())
            for h in range(2):
                P.op("vector", lambda e, h=h, qc=qc, p=p: e.tensor_scalar(
                    zpair[64 * h:64 * h + 64, qc * c.TCS:(qc + 1) * c.TCS],
                    zpair[64 * h:64 * h + 64, qc * c.TCS:(qc + 1) * c.TCS],
                    bvc[64 * h:64 * h + 64, p:p + 1], None, op0=ALU.add),
                    inc=(sv, 1))
            zevac_v = sv.n

        for tb in range(c.TB):
            for dcc in range(c.D // 512):
                bank = ps[(tb * (c.D // 512) + dcc) % 4]
                P.op("tensor", lambda e, b=bank, tb=tb, dcc=dcc, p=p: e.matmul(
                    b[:, 0:512], zpair[:, tb * 128:(tb + 1) * 128],
                    wo_f32[:, dcc * 512:(dcc + 1) * 512],
                    start=True, stop=(p != 0)),
                    waits=[(sv, sv.n)],
                    inc=None if p == 0 else (st_, 1))
                if p == 0:
                    P.op("tensor", lambda e, b=bank, dcc=dcc: e.matmul(
                        b[:, 0:512], onesr, bot[0:1, dcc * 512:(dcc + 1) * 512],
                        start=False, stop=True), inc=(st_, 1))
                col = tb * c.D + dcc * 512
                if p == 0:
                    P.op("vector", lambda e, b=bank, col=col: e.tensor_copy(
                        big2[:, col:col + 512], b[:, 0:512]),
                        waits=[st_.w()], inc=(sv, 1))
                else:
                    P.op("vector", lambda e, b=bank, col=col: e.tensor_tensor(
                        big2[:, col:col + 512], b[:, 0:512],
                        big2[:, col:col + 512], op=ALU.add),
                        waits=[st_.w()], inc=(sv, 1))
        last_wo_use = st_.n

    if "C" not in phases:
        if debug_dump:
            dxf = nc.dram_tensor("dbg_xfm", [128, c.DC * c.T], F32, kind="ExternalOutput").ap()
            dbg = nc.dram_tensor("dbg_big2", [128, c.TB * c.D], F32, kind="ExternalOutput").ap()
            dsb = nc.dram_tensor("dbg_sbig", [128, sb_len], F32, kind="ExternalOutput").ap()
            dsc = nc.dram_tensor("dbg_scr", [128, 2 * c.D], F32, kind="ExternalOutput").ap()
            P.op("sync", lambda e: e.dma_start(out=dxf, in_=xfm),
                 waits=[(sv, sv.n), (ss, ss.n), (st_, st_.n), (sg, sg.n), (sw, sw.n)],
                 inc=(so, 16))
            P.op("sync", lambda e: e.dma_start(out=dbg, in_=big2), inc=(so, 16))
            P.op("sync", lambda e: e.dma_start(out=dsb, in_=sbig), inc=(so, 16))
            P.op("sync", lambda e: e.dma_start(out=dsc, in_=scratch), inc=(so, 16))
            dgt = nc.dram_tensor("dbg_gates", [128, c.TB * c.ER], F32, kind="ExternalOutput").ap()
            P.op("sync", lambda e: e.dma_start(out=dgt, in_=gates), inc=(so, 16))
            dst2 = nc.dram_tensor("dbg_strip", [128, c.TB * 16], F32, kind="ExternalOutput").ap()
            P.op("sync", lambda e: e.dma_start(out=dst2, in_=dstrip), inc=(so, 16))
            P.barrier("sync", waits=[(so, so.n)])

        _finish(nc, P, sems)
        return nc
    rm = scratch[:, 0:c.D]
    xn = scratch[:, c.D:2 * c.D]
    xfm_tb_v = {}
    DH2 = c.D // 2

    def split_tt(dst, a, b_, op):
        P.op("vector", lambda e: e.tensor_tensor(
            dst[:, 0:DH2], a[:, 0:DH2], b_[:, 0:DH2], op=op))
        P.op("gpsimd", lambda e: e.tensor_tensor(
            dst[:, DH2:c.D], a[:, DH2:c.D], b_[:, DH2:c.D], op=op),
            waits=[(sv, sv.n)])

    for tb in range(c.TB):
        P.op("scalar", lambda e, tb=tb: e.dma_start(
            out=resid, in_=dx0[tb * 128:(tb + 1) * 128, :]),
            waits=[(sv, sv.n), (sg, sg.n)], inc=(sx, 16))
        acc_tb = big2[:, tb * c.D:(tb + 1) * c.D]
        P.op("vector", lambda e, a=acc_tb: e.tensor_tensor(
            rm[:, 0:DH2], a[:, 0:DH2], resid[:, 0:DH2], op=ALU.add),
            waits=[(sx, sx.n), (sv, sv.n)], inc=(sv, 1))
        P.op("gpsimd", lambda e, a=acc_tb: e.tensor_tensor(
            rm[:, DH2:c.D], a[:, DH2:c.D], resid[:, DH2:c.D], op=ALU.add),
            waits=[(sx, sx.n), (sv, sv.n)], inc=(sg, 1))
        ln_stats(rm, c.D, waits=[(sg, sg.n)])
        ln_apply(xn, rm, waits=[])
        gw = sg.n
        P.op("gpsimd", lambda e: e.tensor_scalar(
            xn[:, DH2:c.D], rm[:, DH2:c.D], mv2[:, 0:1], rstd,
            op0=ALU.subtract, op1=ALU.mult), waits=[(sv, sv.n)], inc=(sg, 1))
        P.op("vector", lambda e: e.tensor_tensor(
            xn[:, 0:DH2], l2w[:, 0:DH2], xn[:, 0:DH2], op=ALU.mult))
        P.op("gpsimd", lambda e: e.tensor_tensor(
            xn[:, DH2:c.D], l2w[:, DH2:c.D], xn[:, DH2:c.D], op=ALU.mult))
        P.op("vector", lambda e: e.tensor_tensor(
            xn[:, 0:DH2], l2b[:, 0:DH2], xn[:, 0:DH2], op=ALU.add))
        P.op("gpsimd", lambda e: e.tensor_tensor(
            xn[:, DH2:c.D], l2b[:, DH2:c.D], xn[:, DH2:c.D], op=ALU.add))
        P.op("vector", lambda e: e.tensor_tensor(
            rm[:, 0:DH2], xn[:, 0:DH2], rm[:, 0:DH2], op=ALU.add))
        P.op("gpsimd", lambda e: e.tensor_tensor(
            rm[:, DH2:c.D], xn[:, DH2:c.D], rm[:, DH2:c.D], op=ALU.add))
        P.op("vector", lambda e: e.tensor_tensor(
            rm[:, 0:DH2], resid[:, 0:DH2], rm[:, 0:DH2], op=ALU.add),
            inc=(sv, 1))
        P.op("gpsimd", lambda e: e.tensor_tensor(
            rm[:, DH2:c.D], resid[:, DH2:c.D], rm[:, DH2:c.D], op=ALU.add),
            inc=(sg, 1))
        ln_stats(rm, c.D, waits=[(sg, sg.n)])
        ln_apply(big2[:, tb * c.D:(tb + 1) * c.D], rm, waits=[])
        xa_v = sv.n
        for dc in range(c.DC):
            bank = ps[6 + (dc % 2)]
            P.op("tensor", lambda e, b=bank, tb=tb, dc=dc: e.transpose(
                b[:, 0:128], big2[:, tb * c.D + dc * 128: tb * c.D + dc * 128 + 128],
                ident), waits=[(sv, xa_v), (sv, sv.n)], inc=(st_, 1))
            col = dc * c.T + tb * 128
            tmp = scratch[:, c.D + (dc % 2) * 128:c.D + (dc % 2) * 128 + 128]
            P.op("vector", lambda e, b=bank, t=tmp: e.tensor_copy(
                t, b[:, 0:128]), waits=[st_.w(), (sg, tmp_free[dc % 2])],
                inc=(sv, 1))
            P.op("gpsimd", lambda e, t=tmp, col=col: e.tensor_copy(
                xfm_h[:, col:col + 128], t), waits=[sv.w()])
            P.op("gpsimd", lambda e, t=tmp, col=col: e.tensor_tensor(
                xfm_l[:, col:col + 128], t, xfm_h[:, col:col + 128],
                op=ALU.subtract), inc=(sg, 1))
            tmp_free[dc % 2] = sg.n
        xfm_tb_v[tb] = sg.n

    if "D" not in phases:
        if debug_dump:
            dxf = nc.dram_tensor("dbg_xfm", [128, c.DC * c.T], F32, kind="ExternalOutput").ap()
            dbg = nc.dram_tensor("dbg_big2", [128, c.TB * c.D], F32, kind="ExternalOutput").ap()
            dsb = nc.dram_tensor("dbg_sbig", [128, sb_len], F32, kind="ExternalOutput").ap()
            dsc = nc.dram_tensor("dbg_scr", [128, 2 * c.D], F32, kind="ExternalOutput").ap()
            P.op("sync", lambda e: e.dma_start(out=dxf, in_=xfm),
                 waits=[(sv, sv.n), (ss, ss.n), (st_, st_.n), (sg, sg.n), (sw, sw.n)],
                 inc=(so, 16))
            P.op("sync", lambda e: e.dma_start(out=dbg, in_=big2), inc=(so, 16))
            P.op("sync", lambda e: e.dma_start(out=dsb, in_=sbig), inc=(so, 16))
            P.op("sync", lambda e: e.dma_start(out=dsc, in_=scratch), inc=(so, 16))
            dgt = nc.dram_tensor("dbg_gates", [128, c.TB * c.ER], F32, kind="ExternalOutput").ap()
            P.op("sync", lambda e: e.dma_start(out=dgt, in_=gates), inc=(so, 16))
            dst2 = nc.dram_tensor("dbg_strip", [128, c.TB * 16], F32, kind="ExternalOutput").ap()
            P.op("sync", lambda e: e.dma_start(out=dst2, in_=dstrip), inc=(so, 16))
            P.barrier("sync", waits=[(so, so.n)])

        _finish(nc, P, sems)
        return nc
    def emit_router():
        logt_free = ss.n
        for tb in range(c.TB):
            for dc in range(c.DC):
                xo = dc * c.T + tb * 128
                mm3(ps[6][:, 0:c.ER],
                    xfm_h[:, xo:xo + 128], xfm_l[:, xo:xo + 128],
                    rtwP[:, dc * c.ER:(dc + 1) * c.ER],
                    rtwP[:, NRW + dc * c.ER:NRW + (dc + 1) * c.ER],
                    start=(dc == 0), stop=(dc == c.DC - 1),
                    waits=([(sv, sv.n), (sg, sg.n), (ss, logt_free)]
                           if dc == 0 else []),
                    inc=(st_, 1) if dc == c.DC - 1 else None)
            P.op("scalar", lambda e: e.copy(logt, ps[6][:, 0:c.ER]),
                 waits=[st_.w()], inc=(ss, 1))
            logt_free = ss.n
            P.op("vector", lambda e: e.tensor_tensor(logt, rtb, logt, op=ALU.add),
                 waits=[ss.w()], inc=(sv, 1))
            P.op("vector", lambda e: e.drain())
            P.op("vector", lambda e: e.max(m8, logt))
            P.op("vector", lambda e: e.drain())
            P.op("vector", lambda e: e.max_index(i8, m8, logt))
            P.op("vector", lambda e: e.tensor_scalar(
                sc1, m8[:, 1:2], m8[:, 0:1], None, op0=ALU.subtract))
            P.op("vector", lambda e: e.drain(), inc=(sv, 1))
            P.op("scalar", lambda e: e.activation(sc2, sc1, AF.Exp),
                 waits=[sv.w()], inc=(ss, 1))
            P.op("vector", lambda e: e.tensor_scalar(
                sc1, sc2, 1.0, None, op0=ALU.add), waits=[ss.w()])
            P.op("vector", lambda e: e.drain())
            P.op("vector", lambda e: e.reciprocal(w1g, sc1))
            P.op("vector", lambda e: e.drain())
            P.op("vector", lambda e: e.tensor_scalar(
                w2g, w1g, -1.0, 1.0, op0=ALU.mult, op1=ALU.add))
            P.op("vector", lambda e: e.tensor_copy(fi2, i8[:, 0:2]))
            P.op("vector", lambda e: e.drain())
            P.op("vector", lambda e: e.tensor_scalar(
                logt, iot8, fi2[:, 0:1], None, op0=ALU.is_equal))
            P.op("vector", lambda e: e.tensor_scalar(
                m8, iot8, fi2[:, 1:2], None, op0=ALU.is_equal))
            P.op("vector", lambda e: e.drain())
            P.op("vector", lambda e: e.tensor_scalar(
                logt, logt, w1g, None, op0=ALU.mult))
            P.op("vector", lambda e: e.drain())
            P.op("vector", lambda e, tb=tb: e.scalar_tensor_tensor(
                gates[:, tb * c.ER:(tb + 1) * c.ER], m8, w2g, logt,
                op0=ALU.mult, op1=ALU.add), inc=(sv, 1))

    w1P = [wpair(0), wpair(1)]
    w3P = [wpair(2), wpair(3)]
    w2P = [wpair(4), wpair(5)]
    gs = scratch[:, 0:c.TCS]
    h_tmp = scratch[:, c.TCS:2 * c.TCS]

    def hpair(hci):
        hp = sbig[:, hci * c.T:(hci + 1) * c.T].bitcast(mybir.dt.bfloat16)
        return hp[:, 0:c.T], hp[:, c.T:2 * c.T]

    est = {
        "w1_free": [last_kq_use, last_kq_use],
        "w3_free": [last_wv_use, last_wo_use],
        "w2_free": [0, 0],
        "g_free": [0, 0],
        "u_free": [0, 0],
        "htmp_free": 0,
        "h_war": last_wo_use,
        "yb_free": {},
    }
    NTC = c.T // c.TCS

    def emit_expert(ei):
        shared = ei < c.ES
        for hg in range(c.NHG):
            h_sg = {}
            for hci in range(c.HGSZ):
                hc = hg * c.HGSZ + hci
                slot = hc % 2
                hh, hl = hpair(hci)
                w1rd = stream_pair(dw1[ei, :, hc * c.D:(hc + 1) * c.D], c.D,
                                   w1P[slot][0], w1P[slot][1],
                                   pre_wait=(st_, est["w1_free"][slot]))
                w3rd = stream_pair(dw3[ei, :, hc * c.D:(hc + 1) * c.D], c.D,
                                   w3P[slot][0], w3P[slot][1],
                                   pre_wait=(st_, est["w3_free"][slot]))
                for tc in range(NTC):
                    gb = ps[0 + (tc % 2)]
                    ub = ps[2 + (tc % 2)]
                    xrd = xfm_tb_v[min((tc + 1) * (c.TCS // 128), c.TB) - 1]
                    for dc in range(c.DC):
                        xo = dc * c.T + tc * c.TCS
                        mm3(gb[:, 0:c.TCS],
                            w1P[slot][0][:, dc * 128:(dc + 1) * 128],
                            w1P[slot][1][:, dc * 128:(dc + 1) * 128],
                            xfm_h[:, xo:xo + c.TCS], xfm_l[:, xo:xo + c.TCS],
                            start=(dc == 0), stop=(dc == c.DC - 1),
                            waits=([(sg, w1rd), (sg, xrd),
                                    (ss, est["g_free"][tc % 2])]
                                   if dc == 0 else []),
                            inc=(st_, 1) if dc == c.DC - 1 else None)
                    for dc in range(c.DC):
                        xo = dc * c.T + tc * c.TCS
                        mm3(ub[:, 0:c.TCS],
                            w3P[slot][0][:, dc * 128:(dc + 1) * 128],
                            w3P[slot][1][:, dc * 128:(dc + 1) * 128],
                            xfm_h[:, xo:xo + c.TCS], xfm_l[:, xo:xo + c.TCS],
                            start=(dc == 0), stop=(dc == c.DC - 1),
                            waits=([(sg, w3rd), (sv, est["u_free"][tc % 2])]
                                   if dc == 0 else []),
                            inc=(st_, 1) if dc == c.DC - 1 else None)
                    mm_done = st_.n
                    P.op("scalar", lambda e, b=gb: e.activation(
                        gs, b[:, 0:c.TCS], AF.Silu),
                        waits=[(st_, mm_done)], inc=(ss, 1))
                    est["g_free"][tc % 2] = ss.n
                    P.op("vector", lambda e, b=ub: e.tensor_tensor(
                        h_tmp, gs, b[:, 0:c.TCS], op=ALU.mult),
                        waits=[ss.w(), (st_, mm_done), (sg, est["htmp_free"])],
                        inc=(sv, 1))
                    est["u_free"][tc % 2] = sv.n
                    hs = tc * c.TCS
                    P.op("gpsimd", lambda e, hh=hh, hs=hs: e.tensor_copy(
                        hh[:, hs:hs + c.TCS], h_tmp),
                        waits=[sv.w(), (st_, est["h_war"])])
                    P.op("gpsimd", lambda e, hh=hh, hl=hl, hs=hs: e.tensor_tensor(
                        hl[:, hs:hs + c.TCS], h_tmp, hh[:, hs:hs + c.TCS],
                        op=ALU.subtract), inc=(sg, 1))
                    est["htmp_free"] = sg.n
                    h_sg[(hci, tc)] = sg.n
                est["w1_free"][slot] = st_.n
                est["w3_free"][slot] = st_.n
            h_g = sg.n
            w2rd = []
            for hci in range(c.HGSZ):
                hc = hg * c.HGSZ + hci
                w2rd.append(stream_pair(
                    dw2[ei, :, hc * c.D:(hc + 1) * c.D], c.D,
                    w2P[hci][0], w2P[hci][1],
                    pre_wait=(st_, est["w2_free"][hci])))
            for tc2 in range(c.T // c.TC2):
                banks = [ps[4 + i] for i in range(c.TB2 * c.ND2)]
                for hci in range(c.HGSZ):
                    hh, hl = hpair(hci)
                    for tb2 in range(c.TB2):
                        for d2 in range(c.ND2):
                            b = banks[tb2 * c.ND2 + d2]
                            bi = tb2 * c.ND2 + d2
                            tcol = tc2 * c.TC2 + tb2 * 128
                            waits = []
                            if hci == 0:
                                waits = [(sg, w2rd[-1]), (sg, h_g)]
                                if bi in est["yb_free"]:
                                    waits.append((sv, est["yb_free"][bi]))
                            mm3(b[:, 0:c.N2],
                                hh[:, tcol:tcol + 128], hl[:, tcol:tcol + 128],
                                w2P[hci][0][:, d2 * c.N2:(d2 + 1) * c.N2],
                                w2P[hci][1][:, d2 * c.N2:(d2 + 1) * c.N2],
                                start=(hci == 0), stop=(hci == c.HGSZ - 1),
                                waits=waits,
                                inc=(st_, 1) if hci == c.HGSZ - 1 else None)
                y_done = st_.n
                for tb2 in range(c.TB2):
                    gtb = (tc2 * c.TC2) // 128 + tb2
                    for d2 in range(c.ND2):
                        b = banks[tb2 * c.ND2 + d2]
                        bi = tb2 * c.ND2 + d2
                        col = gtb * c.D + d2 * c.N2
                        if shared:
                            P.op("vector", lambda e, b=b, col=col: e.tensor_tensor(
                                big2[:, col:col + c.N2], b[:, 0:c.N2],
                                big2[:, col:col + c.N2], op=ALU.add),
                                waits=[(st_, y_done)], inc=(sv, 1))
                        else:
                            gate = gates[:, gtb * c.ER + (ei - c.ES):
                                         gtb * c.ER + (ei - c.ES) + 1]
                            P.op("vector", lambda e, b=b, col=col, gate=gate:
                                 e.scalar_tensor_tensor(
                                     big2[:, col:col + c.N2], b[:, 0:c.N2], gate,
                                     big2[:, col:col + c.N2],
                                     op0=ALU.mult, op1=ALU.add),
                                 waits=[(st_, y_done)], inc=(sv, 1))
                        est["yb_free"][bi] = sv.n
            for hci in range(c.HGSZ):
                est["w2_free"][hci] = st_.n
            est["h_war"] = st_.n

    for ei in range(c.ES):
        emit_expert(ei)
    emit_router()
    for ei in range(c.ES, c.NE):
        emit_expert(ei)

    for tb in range(c.TB):
        ln_stats(big2[:, tb * c.D:(tb + 1) * c.D], c.D, waits=[(sv, sv.n)])
        ln_apply(rm, big2[:, tb * c.D:(tb + 1) * c.D], waits=[])
        out_v = sv.n
        P.op("sync", lambda e, tb=tb: e.dma_start(
            out=dy[tb * 128:(tb + 1) * 128, :], in_=rm),
            waits=[(sv, out_v)], inc=(so, 16))
        P.barrier("vector", waits=[(so, so.n)])

    P.barrier("sync", waits=[(so, so.n)])

    if debug_dump:
        dxf = nc.dram_tensor("dbg_xfm", [128, c.DC * c.T], F32, kind="ExternalOutput").ap()
        dbg = nc.dram_tensor("dbg_big2", [128, c.TB * c.D], F32, kind="ExternalOutput").ap()
        dsb = nc.dram_tensor("dbg_sbig", [128, sb_len], F32, kind="ExternalOutput").ap()
        dsc = nc.dram_tensor("dbg_scr", [128, 2 * c.D], F32, kind="ExternalOutput").ap()
        P.op("sync", lambda e: e.dma_start(out=dxf, in_=xfm),
             waits=[(sv, sv.n), (ss, ss.n), (st_, st_.n), (sg, sg.n), (sw, sw.n)],
             inc=(so, 16))
        P.op("sync", lambda e: e.dma_start(out=dbg, in_=big2), inc=(so, 16))
        P.op("sync", lambda e: e.dma_start(out=dsb, in_=sbig), inc=(so, 16))
        P.op("sync", lambda e: e.dma_start(out=dsc, in_=scratch), inc=(so, 16))
        P.op("sync", lambda e: e.dma_start(out=dgt, in_=gates), inc=(so, 16))
        P.barrier("sync", waits=[(so, so.n)])

    _finish(nc, P, sems)
    return nc



def stage_inputs(cfg, ins):
    c = cfg
    H, DH, D, HID = c.H, c.DH, c.D, c.HID
    WQ, WK, WV = ins["L0_attn_W_Q"], ins["L0_attn_W_K"], ins["L0_attn_W_V"]
    WO = ins["L0_attn_W_O"]

    def pair_lhsT(W):
        cols = []
        for p in range(c.PAIRS):
            m = np.concatenate([W[2 * p], W[2 * p + 1]], axis=1)
            m = m.reshape(c.DC, 128, 128).transpose(1, 0, 2).reshape(128, c.D)
            cols.append(m)
        return np.ascontiguousarray(np.concatenate(cols, axis=1), dtype=np.float32)

    wk = pair_lhsT(WK)
    wq = pair_lhsT(WQ)
    wv = pair_lhsT(WV)
    WOf = WO.reshape(H * DH, D)
    wo = np.ascontiguousarray(
        WOf.reshape(c.PAIRS, 128, D).transpose(1, 0, 2).reshape(128, c.PAIRS * D),
        dtype=np.float32)
    bk = np.ascontiguousarray(np.stack(
        [np.concatenate([ins["L0_attn_b_K"][2 * p], ins["L0_attn_b_K"][2 * p + 1]])
         for p in range(c.PAIRS)], axis=1), dtype=np.float32)
    bq = np.ascontiguousarray(np.stack(
        [np.concatenate([ins["L0_attn_b_Q"][2 * p], ins["L0_attn_b_Q"][2 * p + 1]])
         for p in range(c.PAIRS)], axis=1), dtype=np.float32)
    bv = np.ascontiguousarray(np.stack(
        [np.concatenate([ins["L0_attn_b_V"][2 * p], ins["L0_attn_b_V"][2 * p + 1]])
         for p in range(c.PAIRS)], axis=1), dtype=np.float32)
    bo = np.ascontiguousarray(ins["L0_attn_b_O"][None, :], dtype=np.float32)
    l1w = np.ascontiguousarray(
        ins["L0_attn_ln1_w"].reshape(c.DC, 128).T, dtype=np.float32)
    l1b = np.ascontiguousarray(
        ins["L0_attn_ln1_b"].reshape(c.DC, 128).T, dtype=np.float32)
    l2w = np.ascontiguousarray(
        np.broadcast_to(ins["L0_attn_ln2_w"][None, :], (128, D)), dtype=np.float32)
    l2b = np.ascontiguousarray(
        np.broadcast_to(ins["L0_attn_ln2_b"][None, :], (128, D)), dtype=np.float32)
    rw = ins["L0_ffn_router_w"]
    rtw = np.ascontiguousarray(
        rw.T.reshape(c.DC, 128, c.ER).transpose(1, 0, 2).reshape(128, c.DC * c.ER),
        dtype=np.float32)
    rtb = np.ascontiguousarray(
        np.broadcast_to(ins["L0_ffn_expert_bias"][None, :], (128, c.ER)),
        dtype=np.float32)

    def stage_w13(w):
        m = w.T
        m = m.reshape(c.DC, 128, c.HC, 128).transpose(1, 2, 0, 3)
        return m.reshape(128, c.HC * c.DC * 128)

    def stage_w2(w):
        m = w.T
        m = m.reshape(c.HC, 128, D).transpose(1, 0, 2)
        return m.reshape(128, c.HC * D)

    e_w1 = [ins["L0_ffn_sw1"][i] for i in range(c.ES)] + \
           [ins["L0_ffn_w1"][i] for i in range(c.ER)]
    e_w3 = [ins["L0_ffn_sw3"][i] for i in range(c.ES)] + \
           [ins["L0_ffn_w3"][i] for i in range(c.ER)]
    e_w2 = [ins["L0_ffn_sw2"][i] for i in range(c.ES)] + \
           [ins["L0_ffn_w2"][i] for i in range(c.ER)]
    w1s = np.ascontiguousarray(np.stack([stage_w13(np.asarray(w)) for w in e_w1]),
                               dtype=np.float32)
    w3s = np.ascontiguousarray(np.stack([stage_w13(np.asarray(w)) for w in e_w3]),
                               dtype=np.float32)
    w2s = np.ascontiguousarray(np.stack([stage_w2(np.asarray(w)) for w in e_w2]),
                               dtype=np.float32)
    return {
        "wk": wk, "wq": wq, "wv": wv, "wo": wo, "bk": bk, "bq": bq, "bv": bv,
        "bo": bo, "ln1w": l1w, "ln1b": l1b, "ln2w": l2w, "ln2b": l2b,
        "rtw": rtw, "rtb": rtb, "w1s": w1s, "w3s": w3s, "w2s": w2s,
    }


def kernel(**inputs):
    cfg = FULL
    x = np.ascontiguousarray(np.asarray(inputs["L0_attn_resid_pre"],
                                        dtype=np.float32))
    B = x.shape[0]
    shared = stage_inputs(cfg, {k: np.asarray(v) for k, v in inputs.items()})
    nc = build_nc(cfg)
    in_maps = []
    for b in range(B):
        m = dict(shared)
        m["x0"] = np.ascontiguousarray(x[b])
        in_maps.append(m)
    res = run_bass_kernel_spmd(nc, in_maps, core_ids=list(range(B)))
    return np.stack([res.results[b]["y"] for b in range(B)], axis=0)
